# version 06
# baseline (speedup 1.0000x reference)
import numpy as np
import ml_dtypes
import concourse.bass as bass
import concourse.mybir as mybir

F32, BF16, I32 = mybir.dt.float32, mybir.dt.bfloat16, mybir.dt.int32
AF = mybir.ActivationFunctionType
ALU = mybir.AluOpType
ENGINES = ("tensor", "vector", "scalar", "sync", "gpsimd")


class Cfg:
    def __init__(self, n_dc=32, n_ic=86, n_qc=12, n_heads=32, n_rc=4, TT=512, NT=2,
                 n_mem=4, dr=64, eps=1e-5):
        self.n_dc, self.n_ic, self.n_qc = n_dc, n_ic, n_qc
        self.n_heads, self.n_rc = n_heads, n_rc
        self.TT, self.NT, self.n_mem, self.dr = TT, NT, n_mem, dr
        self.eps = eps
        self.D = n_dc * 128
        self.I = n_ic * 128
        self.QL = n_qc * 128
        self.KVL = n_rc * 128
        self.T = TT * NT
        self.S = n_mem * self.T
        self.n_tc = self.S // 128
        self.n_hc = n_dc
        self.SBLK = TT
        self.scale = 1.0 / np.sqrt(128.0 + dr)
        self.GA = self.S - 1
        self.GLEN = self.GA + (self.n_tc - 1) * 128 + 1
        self.LSB = [((sb + 1) * n_mem * TT) // 128 for sb in range(NT)]


class Prog:
    def __init__(self, nc):
        self.nc = nc
        self.ops = {e: [] for e in ENGINES}
        self.sems = {}
        self.cnt = {}
        self._guards = []

    def sem(self, name):
        g = self.nc.semaphore(name)
        h = g.__enter__()
        self._guards.append(g)
        self.sems[name] = h
        self.cnt[name] = 0
        return h

    def emit(self, eng, fn):
        self.ops[eng].append(fn)

    def inc(self, eng, fn, sem, n=1):
        h = self.sems[sem]
        self.emit(eng, lambda e, fn=fn, h=h, n=n: fn(e).then_inc(h, n))
        self.cnt[sem] += n
        return self.cnt[sem]

    def wait(self, eng, sem, val):
        if val is None:
            return
        h = self.sems[sem]
        self.emit(eng, lambda e, h=h, val=val: e.wait_ge(h, val))

    def barrier(self):
        self.wait("sync", "sA", self.cnt["sA"])
        self.wait("gpsimd", "sW", self.cnt["sW"])
        vals = {}
        for eng in ENGINES:
            s = "bar_" + eng
            h = self.sems[s]
            self.emit(eng, lambda e, h=h: e.nop().then_inc(h, 1))
            self.cnt[s] += 1
            vals[eng] = self.cnt[s]
        for eng in ENGINES:
            for other in ENGINES:
                if other != eng:
                    self.wait(eng, "bar_" + other, vals[other])

    def replay(self):
        with self.nc.Block() as block:
            block.sync(lambda e: [fn(e) for fn in self.ops["sync"]])
            block.gpsimd(lambda e: [fn(e) for fn in self.ops["gpsimd"]])
            block.tensor(lambda e: [fn(e) for fn in self.ops["tensor"]])
            block.vector(lambda e: [fn(e) for fn in self.ops["vector"]])
            block.scalar(lambda e: [fn(e) for fn in self.ops["scalar"]])

    def close(self):
        for g in reversed(self._guards):
            g.__exit__(None, None, None)


def emit_layernorm(p, cfg, src_ap, dst_ap, sb, ps):
    n_dc, TT, NT = cfg.n_dc, cfg.TT, cfg.NT
    xf, xb, xsq = sb["ln_xf"], sb["ln_xb"], sb["ln_xsq"]
    ab, bb, rowm = sb["ln_ab"], sb["ln_bb"], sb["ln_rows"]
    ones_bf, ones1_f32 = sb["ones_bf"], sb["ones1_f32"]
    NQ = 4
    DQ = n_dc // NQ
    v_out = None
    for ts in range(NT):
        lds, casts, sqs = [], [], []
        for q in range(NQ):
            if v_out is not None and q == 0:
                p.wait("sync", "sV", v_out)
            lds.append(p.inc("sync", lambda e, s=src_ap, ts=ts, q=q: e.dma_start(
                xf[:, q * DQ:(q + 1) * DQ, :], s[ts][:, q * DQ:(q + 1) * DQ, :]), "sA", 16))
        stA = ps[0:1, 6 * 512:6 * 512 + TT]
        stB = ps[0:1, 7 * 512:7 * 512 + TT]
        for q in range(NQ):
            p.wait("vector", "sA", lds[q])
            casts.append(p.inc("vector", lambda e, q=q: e.tensor_copy(
                xb[:, q * DQ:(q + 1) * DQ, :], xf[:, q * DQ:(q + 1) * DQ, :]), "sV"))
            p.wait("scalar", "sA", lds[q])
            sqs.append(p.inc("scalar", lambda e, q=q: e.activation(
                xsq[:, q * DQ:(q + 1) * DQ, :], xf[:, q * DQ:(q + 1) * DQ, :], AF.Square), "sS"))
            p.wait("tensor", "sV", casts[q])
            p.wait("tensor", "sS", sqs[q])
            for d in range(DQ):
                dc = q * DQ + d
                p.emit("tensor", lambda e, dc=dc, stA=stA: e.matmul(
                    stA, ones_bf[:, :], xb[:, dc, :], start=(dc == 0), stop=(dc == n_dc - 1)))
            for d in range(DQ):
                dc = q * DQ + d
                fn = lambda e, dc=dc, stB=stB: e.matmul(
                    stB, ones_bf[:, :], xsq[:, dc, :], start=(dc == 0), stop=(dc == n_dc - 1))
                if dc == n_dc - 1:
                    t_st = p.inc("tensor", fn, "sT")
                else:
                    p.emit("tensor", fn)
        mean = rowm[0:1, 0:TT]
        e2 = rowm[0:1, TT:2 * TT]
        var = rowm[0:1, 2 * TT:3 * TT]
        lnv = rowm[0:1, 3 * TT:4 * TT]
        rstd = rowm[0:1, 4 * TT:5 * TT]
        mb = rowm[0:1, 5 * TT:6 * TT]
        p.wait("vector", "sT", t_st)
        p.emit("vector", lambda e, stA=stA: e.tensor_scalar_mul(mean, stA, 1.0 / cfg.D))
        p.emit("vector", lambda e, stB=stB: e.tensor_scalar_mul(e2, stB, 1.0 / cfg.D))
        p.emit("vector", lambda e: e.tensor_mul(var, mean, mean))
        v_var = p.inc("vector", lambda e: e.tensor_sub(var, e2, var), "sV")
        p.wait("scalar", "sV", v_var)
        p.emit("scalar", lambda e: e.activation(lnv, var, AF.Ln, bias=sb["cst"][0:1, 0:1]))
        s_rstd = p.inc("scalar", lambda e: e.activation(rstd, lnv, AF.Exp, scale=-0.5), "sS")
        p.wait("vector", "sS", s_rstd)
        p.emit("vector", lambda e: e.tensor_mul(mb, mean, rstd))
        v_mb = p.inc("vector", lambda e: e.tensor_scalar_mul(mb, mb, -1.0), "sV")
        p.wait("tensor", "sV", v_mb)
        bcA = ps[:, 4 * 512:4 * 512 + TT]
        bcB = ps[:, 5 * 512:5 * 512 + TT]
        p.emit("tensor", lambda e, bcA=bcA: e.matmul(bcA, ones1_f32[:, :], rstd, start=True, stop=True))
        t_bc = p.inc("tensor", lambda e, bcB=bcB: e.matmul(bcB, ones1_f32[:, :], mb, start=True, stop=True), "sT")
        p.wait("vector", "sT", t_bc)
        p.emit("vector", lambda e, bcA=bcA: e.tensor_copy(ab[:, :], bcA))
        p.inc("vector", lambda e, bcB=bcB: e.tensor_copy(bb[:, :], bcB), "sV")
        for q in range(NQ):
            for d in range(DQ):
                dc = q * DQ + d
                p.emit("vector", lambda e, dc=dc: e.tensor_mul(
                    xb[:, dc, :], xb[:, dc, :], ab[:, :]))
                fn = lambda e, dc=dc: e.tensor_add(xb[:, dc, :], xb[:, dc, :], bb[:, :])
                if d == DQ - 1:
                    v_q = p.inc("vector", fn, "sV")
                else:
                    p.emit("vector", fn)
            if q == NQ - 1:
                v_out = v_q
            p.wait("sync", "sV", v_q)
            p.inc("sync", lambda e, d=dst_ap, ts=ts, q=q: e.dma_start(
                d[ts][:, q * DQ:(q + 1) * DQ, :], xb[:, q * DQ:(q + 1) * DQ, :]), "sA", 16)


def emit_ffn(p, cfg, t, sb, ps, src_name, res_name, dst_name, tag=""):
    n_dc, n_ic, n_hc, TT, NT = cfg.n_dc, cfg.n_ic, cfg.n_hc, cfg.TT, cfg.NT
    xb, guw, dbuf = sb["ffn_xb"], sb["ffn_guw"], sb["ffn_dbuf"]
    gu, gsc, xres, x1t = sb["ffn_gu"], sb["ffn_gsc"], sb["ffn_xres"], sb["ffn_x1t"]
    gs_us, ds_half = sb["gs_us"], sb["ds_half"]
    src, res, dst = t[src_name], t[res_name], t[dst_name]
    HDC = n_dc // 2
    NGW = 3
    QI = (n_ic + 3) // 4
    NQ = (n_ic + QI - 1) // QI
    gfree = {}
    g_evac, u_used = {}, {}
    dcons = {}
    ev_done, st_done = {}, {}

    def emit_gload(k):
        gts, rem = divmod(k, 2 * n_ic)
        ic, half = rem // 2, rem % 2
        if k - NGW >= 0:
            p.wait("gpsimd", "sT", gfree[k - NGW])
        return p.inc("gpsimd", lambda e, ic=ic, half=half, k=k: e.dma_start(
            guw[:, k % NGW, :, :, :], t["guw_t"][ic, :, :, half * HDC:(half + 1) * HDC, :]), "sW", 16)

    def emit_dload(k):
        dts, rem = divmod(k, n_hc * NQ)
        hc, q = rem // NQ, rem % NQ
        lo = q * QI
        n_t = min(QI, n_ic - lo)
        assert n_t > 0
        if k - 2 >= 0:
            p.wait("gpsimd", "sT", dcons[k - 2])
        return p.inc("gpsimd", lambda e, hc=hc, lo=lo, n_t=n_t, k=k: e.dma_start(
            dbuf[:, k % 2, 0:n_t, :], t["down_t"][hc][:, lo:lo + n_t, :]), "sW", 16)

    gready, dready = {}, {}
    for ts in range(NT):
        if ts > 1:
            p.wait("sync", "sT", gfree[2 * ((ts - 1) * n_ic) - 1])
        a_x = p.inc("sync", lambda e, ts=ts: e.dma_start(xb[:, ts % 2, :, :], src[ts]), "sA", 16)
        if "dbg_xb2" + tag in t:
            p.wait("sync", "sA", a_x)
            a_x = p.inc("sync", lambda e, ts=ts: e.dma_start(t["dbg_xb2" + tag][ts], xb[:, ts % 2, :, :]), "sA", 16)
        for ic in range(n_ic):
            gic = ts * n_ic + ic
            for half in range(2):
                gready[2 * gic + half] = emit_gload(2 * gic + half)
            if ic == n_ic - 3:
                bk = ts * n_hc * NQ
                dready[bk] = emit_dload(bk)
                dready[bk + 1] = emit_dload(bk + 1)
            par = ic % 2
            pg = ps[:, par * 512:par * 512 + TT]
            pu = ps[:, (2 + par) * 512:(2 + par) * 512 + TT]
            if gic - 2 >= 0:
                p.wait("tensor", "sS", g_evac[gic - 2])
                p.wait("tensor", "sV", u_used[gic - 2])
            if ic == 0:
                p.wait("tensor", "sA", a_x)
            for half in range(2):
                k = 2 * gic + half
                gsl = k % NGW
                p.wait("tensor", "sW", gready[k])
                for d in range(HDC):
                    dc = half * HDC + d
                    p.emit("tensor", lambda e, gsl=gsl, d=d, dc=dc, pg=pg, ts=ts: e.matmul(
                        pg, guw[:, gsl, 0, d, :], xb[:, ts % 2, dc, :], start=(dc == 0), stop=(dc == n_dc - 1)))
                for d in range(HDC):
                    dc = half * HDC + d
                    fnu = lambda e, gsl=gsl, d=d, dc=dc, pu=pu, ts=ts: e.matmul(
                        pu, guw[:, gsl, 1, d, :], xb[:, ts % 2, dc, :], start=(dc == 0), stop=(dc == n_dc - 1))
                    if d == HDC - 1:
                        gfree[k] = p.inc("tensor", fnu, "sT")
                    else:
                        p.emit("tensor", fnu)
            p.wait("scalar", "sT", gfree[2 * gic + 1])
            if gic - 2 >= 0:
                p.wait("scalar", "sV", u_used[gic - 2])
            g_evac[gic] = p.inc("scalar", lambda e, ic=ic, par=par, pg=pg: e.activation(
                gsc[:, par, :], pg, AF.Copy, scale=gs_us[:, ic:ic + 1]), "sS")
            p.wait("vector", "sS", g_evac[gic])
            u_used[gic] = p.inc("vector", lambda e, ic=ic, par=par, pu=pu: e.tensor_mul(
                gu[:, ic, :], pu, gsc[:, par, :]), "sV")
        p.wait("tensor", "sV", u_used[ts * n_ic + n_ic - 1])
        if "dbg_gu" + tag in t:
            p.wait("sync", "sV", u_used[ts * n_ic + n_ic - 1])
            p.inc("sync", lambda e, ts=ts: e.dma_start(t["dbg_gu" + tag][ts], gu[:, :, :]), "sA", 16)
            p.inc("sync", lambda e, ts=ts: e.dma_start(t["dbg_xb" + tag][ts], xb[:, ts % 2, :, :]), "sA", 16)
        base_k = ts * n_hc * NQ

        if "dbg_db" + tag in t and ts == 0:
            p.wait("sync", "sW", dready[base_k])
            p.inc("sync", lambda e: e.dma_start(t["dbg_db" + tag][:, :, :], dbuf[:, 0, :, :]), "sA", 16)
        for hc in range(n_hc):
            ghc = ts * n_hc + hc
            if ghc - 2 >= 0:
                p.wait("sync", "sV", ev_done[ghc - 2])
            a_r = p.inc("sync", lambda e, ts=ts, hc=hc: e.dma_start(
                xres[:, hc % 2, :], res[ts][:, hc, :]), "sA", 16)
            py = ps[:, (4 + hc % 2) * 512:(4 + hc % 2) * 512 + TT]
            if ghc - 2 >= 0:
                p.wait("tensor", "sV", ev_done[ghc - 2])
            for ic in range(n_ic):
                q = ic // QI
                kq = base_k + hc * NQ + q
                if ic % QI == 0:
                    p.wait("tensor", "sW", dready[kq])
                fn = lambda e, ic=ic, kq=kq, py=py: e.matmul(
                    py, dbuf[:, kq % 2, ic % QI, :], gu[:, ic, :],
                    start=(ic == 0), stop=(ic == n_ic - 1))
                if (ic + 1) % QI == 0 or ic == n_ic - 1:
                    dcons[kq] = p.inc("tensor", fn, "sT")
                    nk = kq + 2
                    if nk < base_k + n_hc * NQ and nk not in dready:
                        dready[nk] = emit_dload(nk)
                else:
                    p.emit("tensor", fn)
            t_y = dcons[base_k + hc * NQ + NQ - 1]
            p.wait("vector", "sT", t_y)
            p.wait("vector", "sA", a_r)
            if ghc - 2 >= 0:
                p.wait("vector", "sA", st_done[ghc - 2])
            if "dbg_y" + tag in t:
                p.wait("scalar", "sT", t_y)
                s_dy = p.inc("scalar", lambda e, py=py: e.activation(sb["ffn_dbgy"][:, :], py, AF.Copy), "sS")
                p.wait("sync", "sS", s_dy)
                a_dy = p.inc("sync", lambda e, ts=ts, hc=hc: e.dma_start(t["dbg_y" + tag][ts][:, hc, :], sb["ffn_dbgy"][:, :]), "sA", 16)
                p.wait("scalar", "sA", a_dy)
            ev_done[ghc] = p.inc("vector", lambda e, hc=hc, py=py: e.scalar_tensor_tensor(
                x1t[:, hc % 2, :], py, ds_half[:, hc:hc + 1], xres[:, hc % 2, :],
                ALU.mult, ALU.add), "sV")
            p.wait("sync", "sV", ev_done[ghc])
            st_done[ghc] = p.inc("sync", lambda e, ts=ts, hc=hc: e.dma_start(
                dst[ts][:, hc, :], x1t[:, hc % 2, :]), "sA", 16)


def emit_kv(p, cfg, t, sb, ps):
    n_dc, n_rc, TT, NT = cfg.n_dc, cfg.n_rc, cfg.TT, cfg.NT
    xb, wkv, wkr = sb["kv_xb"], sb["kv_wkv"], sb["kv_wkr"]
    cvt, tmt, krt, rtmp = sb["kv_cvt"], sb["kv_tmt"], sb["kv_krt"], sb["kv_rtmp"]
    ctab, stab = sb["cos_q"], sb["sin_q"]
    w_kv = p.inc("gpsimd", lambda e: e.dma_start(wkv[:, :, :], t["wkv_fm"][:, :, :]), "sW", 16)
    w_kr = p.inc("gpsimd", lambda e: e.dma_start(wkr[:, :, :], t["wkr_fm"][:, :, :]), "sW", 16)
    cv_evac, cv_st, tm_evac, tm_st = {}, {}, {}, {}
    t_tm_last = None
    v_kr = None
    for ts in range(NT):
        if ts > 0:
            p.wait("sync", "sT", t_tm_last)
        a_x = p.inc("sync", lambda e, ts=ts: e.dma_start(xb[:, :, :], t["xln2"][ts]), "sA", 16)
        p.wait("tensor", "sA", a_x)
        if ts == 0:
            p.wait("tensor", "sW", w_kv)
            p.wait("tensor", "sW", w_kr)
        for rc in range(n_rc):
            pb = ps[:, (rc % 2) * 512:(rc % 2) * 512 + TT]
            gi = ts * n_rc + rc
            if gi - 2 in cv_evac:
                p.wait("tensor", "sS", cv_evac[gi - 2])
            for dc in range(n_dc):
                fn = lambda e, rc=rc, dc=dc, pb=pb: e.matmul(
                    pb, wkv[:, dc, rc * 128:(rc + 1) * 128], xb[:, dc, :],
                    start=(dc == 0), stop=(dc == n_dc - 1))
                if dc == n_dc - 1:
                    t_cv = p.inc("tensor", fn, "sT")
                else:
                    p.emit("tensor", fn)
            p.wait("scalar", "sT", t_cv)
            if gi - 1 in cv_st:
                p.wait("scalar", "sA", cv_st[gi - 1])
            cv_evac[gi] = p.inc("scalar", lambda e, pb=pb: e.activation(cvt[:, :], pb, AF.Copy), "sS")
            p.wait("sync", "sS", cv_evac[gi])
            cv_st[gi] = p.inc("sync", lambda e, rc=rc, ts=ts: e.dma_start(
                t["kv_contrib"][rc * 128:(rc + 1) * 128, ts * TT:(ts + 1) * TT], cvt[:, :]), "sA", 16)
        pk1 = ps[0:32, 2 * 512:2 * 512 + TT]
        pk2 = ps[0:32, 5 * 512:5 * 512 + TT]
        if v_kr is not None:
            p.wait("tensor", "sV", v_kr)
        for dc in range(n_dc):
            p.emit("tensor", lambda e, dc=dc, pk1=pk1: e.matmul(
                pk1, wkr[:, dc, 0:32], xb[:, dc, :], start=(dc == 0), stop=(dc == n_dc - 1)))
            fn = lambda e, dc=dc, pk2=pk2: e.matmul(
                pk2, wkr[:, dc, 32:64], xb[:, dc, :], start=(dc == 0), stop=(dc == n_dc - 1))
            if dc == n_dc - 1:
                t_kr = p.inc("tensor", fn, "sT")
            else:
                p.emit("tensor", fn)
        c_sl = ctab[:, ts * TT:(ts + 1) * TT]
        s_sl = stab[:, ts * TT:(ts + 1) * TT]
        k0 = krt[0:32, 0, ts * TT:(ts + 1) * TT]
        k1 = krt[0:32, 1, ts * TT:(ts + 1) * TT]
        ra = rtmp[0:32, 0, :]
        rb = rtmp[0:32, 1, :]
        p.wait("vector", "sT", t_kr)
        p.emit("vector", lambda e, pk1=pk1, c_sl=c_sl, ra=ra: e.tensor_mul(ra, pk1, c_sl))
        p.emit("vector", lambda e, pk2=pk2, s_sl=s_sl, rb=rb: e.tensor_mul(rb, pk2, s_sl))
        p.emit("vector", lambda e, k0=k0, ra=ra, rb=rb: e.tensor_sub(k0, ra, rb))
        p.emit("vector", lambda e, pk1=pk1, s_sl=s_sl, ra=ra: e.tensor_mul(ra, pk1, s_sl))
        p.emit("vector", lambda e, pk2=pk2, c_sl=c_sl, rb=rb: e.tensor_mul(rb, pk2, c_sl))
        v_kr = p.inc("vector", lambda e, k1=k1, ra=ra, rb=rb: e.tensor_add(k1, ra, rb), "sV")
        p.wait("sync", "sV", v_kr)
        p.inc("sync", lambda e, ts=ts: e.dma_start(
            t["kv_contrib"][n_rc * 128:n_rc * 128 + 32, ts * TT:(ts + 1) * TT],
            krt[0:32, 0, ts * TT:(ts + 1) * TT]), "sA", 16)
        p.inc("sync", lambda e, ts=ts: e.dma_start(
            t["kv_contrib"][n_rc * 128 + 32:n_rc * 128 + 64, ts * TT:(ts + 1) * TT],
            krt[0:32, 1, ts * TT:(ts + 1) * TT]), "sA", 16)
        for tk in range(TT // 128):
            gi = ts * (TT // 128) + tk
            pb = ps[:, (3 + gi % 2) * 512:(3 + gi % 2) * 512 + cfg.KVL]
            if gi - 2 in tm_evac:
                p.wait("tensor", "sS", tm_evac[gi - 2])
            for dc in range(n_dc):
                fn = lambda e, tk=tk, dc=dc, pb=pb: e.matmul(
                    pb, xb[:, dc, tk * 128:(tk + 1) * 128], wkv[:, dc, :],
                    start=(dc == 0), stop=(dc == n_dc - 1))
                if dc == n_dc - 1:
                    t_tm = p.inc("tensor", fn, "sT")
                else:
                    p.emit("tensor", fn)
            t_tm_last = t_tm
            p.wait("scalar", "sT", t_tm)
            if gi - 1 in tm_st:
                p.wait("scalar", "sA", tm_st[gi - 1])
            tm_evac[gi] = p.inc("scalar", lambda e, pb=pb: e.activation(tmt[:, :], pb, AF.Copy), "sS")
            p.wait("sync", "sS", tm_evac[gi])
            tm_st[gi] = p.inc("sync", lambda e, ts=ts, tk=tk: e.dma_start(
                t["tm_contrib"][ts * TT + tk * 128:ts * TT + (tk + 1) * 128, :], tmt[:, :]), "sA", 16)
    p.wait("gpsimd", "sA", p.cnt["sA"])
    if cfg.n_mem == 1:
        p.inc("gpsimd", lambda e: e.dma_start(t["kv_gath"].ap()[:, :], t["kv_contrib"][:, :]), "sW", 16)
        p.inc("gpsimd", lambda e: e.dma_start(t["tm_gath"].ap()[:, :], t["tm_contrib"][:, :]), "sW", 16)
        p.wait("gpsimd", "sW", p.cnt["sW"])
    else:
        n_groups = getattr(cfg, "n_cores", 8) // cfg.n_mem
        groups = [[g * cfg.n_mem + m for m in range(cfg.n_mem)] for g in range(n_groups)]
        p.inc("gpsimd", lambda e, groups=groups: e.collective_compute(
            "AllGather", ALU.bypass, replica_groups=groups,
            ins=[t["kv_contrib"].opt()], outs=[t["kv_gath"].ap().opt()]), "sC1", 1)
        p.inc("gpsimd", lambda e, groups=groups: e.collective_compute(
            "AllGather", ALU.bypass, replica_groups=groups,
            ins=[t["tm_contrib"].opt()], outs=[t["tm_gath"].ap().opt()]), "sC2", 1)


def emit_load_gathered(p, cfg, t, sb, ps):
    n_rc, T, n_mem = cfg.n_rc, cfg.T, cfg.n_mem
    CK = n_rc * 128 + 64
    ckv, tm, kr = sb["at_ckv"], sb["at_tm"], sb["at_kr"]
    p.inc("gpsimd", lambda e: e.dma_start(sb["G"][:, :], t["G_h"][:, :]), "sW", 16)
    TT = cfg.TT
    if cfg.n_mem > 1:
        p.wait("sync", "sC1", 1)
    for m in range(n_mem):
        for ts in range(cfg.NT):
            gpos = ts * n_mem * TT + m * TT
            for rc in range(n_rc):
                p.inc("sync", lambda e, m=m, rc=rc, ts=ts, gpos=gpos: e.dma_start(
                    ckv[:, rc, gpos:gpos + TT],
                    t["kv_gath"].ap()[m * CK + rc * 128:m * CK + (rc + 1) * 128, ts * TT:(ts + 1) * TT]), "sA", 16)
            p.inc("sync", lambda e, m=m, ts=ts, gpos=gpos: e.dma_start(
                kr[0:32, 0, gpos:gpos + TT],
                t["kv_gath"].ap()[m * CK + n_rc * 128:m * CK + n_rc * 128 + 32, ts * TT:(ts + 1) * TT]), "sA", 16)
            p.inc("sync", lambda e, m=m, ts=ts, gpos=gpos: e.dma_start(
                kr[0:32, 1, gpos:gpos + TT],
                t["kv_gath"].ap()[m * CK + n_rc * 128 + 32:m * CK + n_rc * 128 + 64, ts * TT:(ts + 1) * TT]), "sA", 16)
    if cfg.n_mem > 1:
        p.wait("sync", "sC2", 1)
    TPC = TT // 128
    for ci in range(cfg.n_mem * cfg.T // 128):
        m, loc = divmod(ci, cfg.T // 128)
        ts, sub = divmod(loc, TPC)
        g = ts * n_mem * TPC + m * TPC + sub
        p.inc("sync", lambda e, ci=ci, g=g: e.dma_start(
            tm[:, g, :], t["tm_gath"].ap()[ci * 128:(ci + 1) * 128, :]), "sA", 16)
    a_keys = p.cnt["sA"]
    p.inc("vector", lambda e: e.memset(kr[32:33, 1, :], -1.0), "sV")
    p.inc("vector", lambda e: e.memset(sb["mk2"][:, :], 0.0), "sV")
    for eng in ("scalar", "vector", "tensor"):
        p.wait(eng, "sA", a_keys)
    p.wait("gpsimd", "sW", p.cnt["sW"])


def emit_maxk2(p, cfg, sb, ps):
    n_rc, T, n_mem = cfg.n_rc, cfg.T, cfg.n_mem
    ckv, kr = sb["at_ckv"], sb["at_kr"]
    sq, nrow, mk2 = sb["at_sq"], sb["at_nrow"], sb["mk2"]
    ones_bf = sb["ones_bf"]
    nch = (n_mem * T) // 512
    pn = ps[0:1, 2 * 512:2 * 512 + 512]
    v_ones = p.cnt["sV"]
    p.wait("tensor", "sV", v_ones)
    sq_done, mm_done = {}, {}
    v_mk = None
    k = 0
    for ch in range(nch):
        sl = slice(ch * 512, (ch + 1) * 512)
        srcs = [(ckv[:, rc, sl], 128) for rc in range(n_rc)] + [(kr[0:32, 0, sl], 32), (kr[0:32, 1, sl], 32)]
        nsrc = len(srcs)
        for i, (src, npart) in enumerate(srcs):
            slot = k % 2
            if k - 2 in mm_done:
                p.wait("scalar", "sT", mm_done[k - 2])
            sq_done[k] = p.inc("scalar", lambda e, src=src, npart=npart, slot=slot: e.activation(
                sq[0:npart, slot, :], src, AF.Square), "sS")
            p.wait("tensor", "sS", sq_done[k])
            if i == 0 and v_mk is not None:
                p.wait("tensor", "sV", v_mk)
            mm_done[k] = p.inc("tensor", lambda e, npart=npart, slot=slot, i=i, nsrc=nsrc: e.matmul(
                pn, ones_bf[0:npart, :], sq[0:npart, slot, :],
                start=(i == 0), stop=(i == nsrc - 1)), "sT")
            k += 1
        p.wait("vector", "sT", mm_done[k - 1])
        p.emit("vector", lambda e: e.reduce_max(nrow[0:1, 0:1], pn, axis=mybir.AxisListType.X))
        v_mk = p.inc("vector", lambda e: e.tensor_max(mk2[:, :], mk2[:, :], nrow[0:1, 0:1]), "sV")
    return v_mk


def emit_attention(p, cfg, t, sb, ps):
    n_dc, n_qc, n_rc, n_heads = cfg.n_dc, cfg.n_qc, cfg.n_rc, cfg.n_heads
    TT, NT, n_tc, SBLK = cfg.TT, cfg.NT, cfg.n_tc, cfg.SBLK
    xbq, attn = sb["at_xbq"], sb["at_attn"]
    qdbuf, obuf = sb["at_qdbuf"], sb["at_obuf"]
    nbuf, rbuf, ukbuf, vbuf = sb["at_nbuf"], sb["at_rbuf"], sb["at_ukbuf"], sb["at_vbuf"]
    cq, qn, qlat, qrope = sb["at_cq"], sb["at_qn"], sb["at_qlat"], sb["at_qrope"]
    probs, olb, invb, iden = sb["at_probs"], sb["at_olb"], sb["at_invb"], sb["at_iden"]
    sq, rtmp = sb["at_sq"], sb["kv_rtmp"]
    xres, x2t = sb["at_xres"], sb["at_x2t"]
    ckv, tm, kr = sb["at_ckv"], sb["at_tm"], sb["at_kr"]
    G, mk2 = sb["G"], sb["mk2"]
    ctab, stab = sb["cos_q"], sb["sin_q"]
    ones_bf, ones1_f32 = sb["ones_bf"], sb["ones1_f32"]
    hw, wo, cqmm, cqev, at_ev = {}, {}, {}, {}, {}
    wo_ev, wo_st = {}, {}
    pv_done = {}
    v_iden_prev = None
    pending_tail = None
    prev_exp_last = None
    qn_ev = None
    qr_rot = None
    sc_last = None
    first_qnorm = True
    for sb_i in range(NT):
        if (sb_i - 1, n_dc - 1) in wo:
            p.wait("sync", "sT", wo[(sb_i - 1, n_dc - 1)])
        a_x = p.inc("sync", lambda e, sb_i=sb_i: e.dma_start(xbq[:, :, :], t["xln2"][sb_i]), "sA", 16)
        for qc in range(n_qc):
            par = qc % 2
            gi = (sb_i, qc)
            if (sb_i, qc - 2) in cqmm:
                p.wait("gpsimd", "sT", cqmm[(sb_i, qc - 2)])
            elif sb_i > 0 and (sb_i - 1, n_dc - 1) in wo:
                p.wait("gpsimd", "sT", wo[(sb_i - 1, n_dc - 1)])
            w_q = p.inc("gpsimd", lambda e, qc=qc, par=par: e.dma_start(
                qdbuf[:, par, :, :], t["wqd_t"][qc]), "sW", 16)
            p.wait("tensor", "sW", w_q)
            if qc == 0:
                p.wait("tensor", "sA", a_x)
            pb = ps[:, par * 512:par * 512 + SBLK]
            if (sb_i, qc - 2) in cqev:
                p.wait("tensor", "sS", cqev[(sb_i, qc - 2)])
            elif qc < 2 and sb_i > 0:
                p.wait("tensor", "sS", at_ev[sb_i * n_heads - 1])
            for dc in range(n_dc):
                fn = lambda e, qc=qc, dc=dc, par=par, pb=pb: e.matmul(
                    pb, qdbuf[:, par, dc, :], xbq[:, dc, :], start=(dc == 0), stop=(dc == n_dc - 1))
                if dc == n_dc - 1:
                    cqmm[gi] = p.inc("tensor", fn, "sT")
                else:
                    p.emit("tensor", fn)
            p.wait("scalar", "sT", cqmm[gi])
            cqev[gi] = p.inc("scalar", lambda e, qc=qc, pb=pb: e.activation(cq[:, qc, :], pb, AF.Copy), "sS")
        if sb_i == 0:
            emit_load_gathered(p, cfg, t, sb, ps)
            v_mk = emit_maxk2(p, cfg, sb, ps)
        for h in range(n_heads):
            hg = sb_i * n_heads + h
            hp = h % 2
            if hg - 2 in hw:
                p.wait("gpsimd", "sT", hw[hg - 2])
            if h < 2:
                p.wait("gpsimd", "sT", cqmm[(sb_i, n_qc - 1)])
            w_n = p.inc("gpsimd", lambda e, h=h, hp=hp: e.dma_start(nbuf[:, hp, :, :], t["wqun_t"][h]), "sW", 16)
            w_r = p.inc("gpsimd", lambda e, h=h, hp=hp: e.dma_start(rbuf[:, hp, :, :], t["wqur_t"][h]), "sW", 16)
            w_k = p.inc("gpsimd", lambda e, h=h, hp=hp: e.dma_start(ukbuf[:, hp, :], t["wuk_fm"][h]), "sW", 16)
            w_v = p.inc("gpsimd", lambda e, h=h, hp=hp: e.dma_start(vbuf[:, hp, :, :], t["wuv_tt"][h]), "sW", 16)
            p.wait("tensor", "sW", w_n)
            p.wait("tensor", "sS", cqev[(sb_i, n_qc - 1)])
            if qn_ev is not None:
                p.wait("tensor", "sS", qn_ev)
            if prev_exp_last is not None:
                p.wait("tensor", "sS", prev_exp_last)
            pqn = ps[:, 0:SBLK]
            for qc in range(n_qc):
                fn = lambda e, qc=qc, hp=hp, pqn=pqn: e.matmul(
                    pqn, nbuf[:, hp, qc, :], cq[:, qc, :], start=(qc == 0), stop=(qc == n_qc - 1))
                if qc == n_qc - 1:
                    t_qn = p.inc("tensor", fn, "sT")
                else:
                    p.emit("tensor", fn)
            p.wait("scalar", "sT", t_qn)
            qn_ev = p.inc("scalar", lambda e, pqn=pqn: e.activation(qn[:, :], pqn, AF.Copy), "sS")
            p.wait("tensor", "sW", w_r)
            if qr_rot is not None:
                p.wait("tensor", "sV", qr_rot)
            p.wait("tensor", "sS", qn_ev)
            pq1 = ps[0:32, 512:512 + SBLK]
            pq2 = ps[0:32, 0:SBLK]
            for qc in range(n_qc):
                p.emit("tensor", lambda e, qc=qc, hp=hp, pq1=pq1: e.matmul(
                    pq1, rbuf[:, hp, qc, 0:32], cq[:, qc, :], start=(qc == 0), stop=(qc == n_qc - 1)))
                fn = lambda e, qc=qc, hp=hp, pq2=pq2: e.matmul(
                    pq2, rbuf[:, hp, qc, 32:64], cq[:, qc, :], start=(qc == 0), stop=(qc == n_qc - 1))
                if qc == n_qc - 1:
                    t_qr = p.inc("tensor", fn, "sT")
                else:
                    p.emit("tensor", fn)
            c_sl = ctab[:, sb_i * SBLK:(sb_i + 1) * SBLK]
            s_sl = stab[:, sb_i * SBLK:(sb_i + 1) * SBLK]
            ra = rtmp[0:32, 0, 0:SBLK]
            rb = rtmp[0:32, 1, 0:SBLK]
            p.wait("vector", "sT", t_qr)
            if sc_last is not None:
                p.wait("vector", "sT", sc_last)
            p.emit("vector", lambda e, pq1=pq1, c_sl=c_sl, ra=ra: e.tensor_mul(ra, pq1, c_sl))
            p.emit("vector", lambda e, pq2=pq2, s_sl=s_sl, rb=rb: e.tensor_mul(rb, pq2, s_sl))
            p.emit("vector", lambda e, ra=ra, rb=rb: e.tensor_sub(qrope[0:32, 0, :], ra, rb))
            p.emit("vector", lambda e, pq1=pq1, s_sl=s_sl, ra=ra: e.tensor_mul(ra, pq1, s_sl))
            p.emit("vector", lambda e, pq2=pq2, c_sl=c_sl, rb=rb: e.tensor_mul(rb, pq2, c_sl))
            qr_rot = p.inc("vector", lambda e, ra=ra, rb=rb: e.tensor_add(qrope[0:32, 1, :], ra, rb), "sV")
            p.wait("tensor", "sW", w_k)
            p.wait("tensor", "sS", qn_ev)
            ql_ev = [None] * n_rc
            for rc in range(n_rc):
                pb = ps[:, (rc % 2) * 512:(rc % 2) * 512 + SBLK]
                if rc == 0:
                    p.wait("tensor", "sV", qr_rot)
                if rc >= 2:
                    p.wait("tensor", "sS", ql_ev[rc - 2])
                t_ql = p.inc("tensor", lambda e, rc=rc, hp=hp, pb=pb: e.matmul(
                    pb, ukbuf[:, hp, rc * 128:(rc + 1) * 128], qn[:, :], start=True, stop=True), "sT")
                p.wait("scalar", "sT", t_ql)
                ql_ev[rc] = p.inc("scalar", lambda e, rc=rc, pb=pb: e.activation(qlat[:, rc, :], pb, AF.Copy), "sS")
            if pending_tail is not None:
                pending_tail()
                pending_tail = None
            if first_qnorm:
                p.wait("tensor", "sV", v_mk)
                first_qnorm = False
            elif v_iden_prev is not None:
                p.wait("tensor", "sV", v_iden_prev)
            qnorm_mm = {}
            for rc in range(n_rc):
                slot = rc % 2
                if rc - 2 >= 0:
                    p.wait("scalar", "sT", qnorm_mm[rc - 2])
                s_sq = p.inc("scalar", lambda e, rc=rc, slot=slot: e.activation(
                    sq[:, slot, 0:SBLK], qlat[:, rc, :], AF.Square), "sS")
                p.wait("tensor", "sS", s_sq)
                qnorm_mm[rc] = p.inc("tensor", lambda e, rc=rc, slot=slot: e.matmul(
                    ps[0:1, 2 * 512:2 * 512 + SBLK], ones_bf[:, :], sq[:, slot, 0:SBLK],
                    start=(rc == 0), stop=False), "sT")
            p.wait("scalar", "sV", qr_rot)
            s_sqr = p.inc("scalar", lambda e: e.activation(
                sq[0:32, 2, 0:SBLK], qrope[0:32, 0, :], AF.Square), "sS")
            s_sqr2 = p.inc("scalar", lambda e: e.activation(
                sq[32:64, 2, 0:SBLK], qrope[0:32, 1, :], AF.Square), "sS") if False else None
            p.wait("tensor", "sS", s_sqr)
            p.inc("tensor", lambda e: e.matmul(
                ps[0:1, 2 * 512:2 * 512 + SBLK], ones_bf[0:32, :], sq[0:32, 2, 0:SBLK],
                start=False, stop=False), "sT")
            p.wait("scalar", "sT", p.cnt["sT"])
            s_sqr3 = p.inc("scalar", lambda e: e.activation(
                sq[0:32, 2, 0:SBLK], qrope[0:32, 1, :], AF.Square), "sS")
            p.wait("tensor", "sS", s_sqr3)
            t_n2 = p.inc("tensor", lambda e: e.matmul(
                ps[0:1, 2 * 512:2 * 512 + SBLK], ones_bf[0:32, :], sq[0:32, 2, 0:SBLK],
                start=False, stop=True), "sT")
            p.wait("vector", "sT", t_n2)
            v_n2 = p.inc("vector", lambda e: e.tensor_scalar(
                iden[0:1, SBLK:2 * SBLK], ps[0:1, 2 * 512:2 * 512 + SBLK], mk2[0:1, 0:1], None, ALU.mult), "sV")
            p.wait("scalar", "sV", v_n2)
            p.emit("scalar", lambda e: e.activation(iden[0:1, 0:SBLK], iden[0:1, SBLK:2 * SBLK], AF.Ln))
            s_brow0 = p.inc("scalar", lambda e: e.activation(
                sb["at_brow"][0:1, :], iden[0:1, 0:SBLK], AF.Exp, scale=0.5, bias=sb["cst"][0:1, 1:2]), "sS")
            p.wait("sync", "sS", s_brow0)
            a_brow = p.inc("sync", lambda e: e.dma_start(qrope[32:33, 1, :], sb["at_brow"][0:1, :]), "sA", 16)
            s_brow = a_brow
            exp_done = [None] * n_tc
            brow_tok = s_brow

            n_run = cfg.LSB[sb_i]

            def emit_pv(tc):
                p.wait("tensor", "sS", exp_done[tc])
                p.emit("tensor", lambda e, tc=tc, n_run=n_run: e.matmul(
                    ps[0:1, 2 * 512:2 * 512 + SBLK], ones_bf[:, :], probs[:, tc % 4, :],
                    start=(tc == 0), stop=(tc == n_run - 1)))
                for rc in range(n_rc):
                    fn = lambda e, tc=tc, rc=rc, n_run=n_run: e.matmul(
                        ps[:, (3 + rc) * 512:(3 + rc) * 512 + SBLK], tm[:, tc, rc * 128:(rc + 1) * 128],
                        probs[:, tc % 4, :], start=(tc == 0), stop=(tc == n_run - 1))
                    if rc == n_rc - 1:
                        pv_done[(hg, tc)] = p.inc("tensor", fn, "sT")
                    else:
                        p.emit("tensor", fn)

            for tc in range(n_run):
                pb = ps[:, (tc % 2) * 512:(tc % 2) * 512 + SBLK]
                if tc >= 2:
                    p.wait("tensor", "sS", exp_done[tc - 2])
                for rc in range(n_rc):
                    p.emit("tensor", lambda e, tc=tc, rc=rc, pb=pb: e.matmul(
                        pb, ckv[:, rc, tc * 128:(tc + 1) * 128], qlat[:, rc, :],
                        start=(rc == 0), stop=False))
                if tc == 0:
                    p.wait("tensor", "sA", brow_tok)
                p.emit("tensor", lambda e, tc=tc, pb=pb: e.matmul(
                    pb, kr[0:32, 0, tc * 128:(tc + 1) * 128], qrope[0:32, 0, :], start=False, stop=False))
                sc_stop = p.inc("tensor", lambda e, tc=tc, pb=pb: e.matmul(
                    pb, kr[0:33, 1, tc * 128:(tc + 1) * 128], qrope[0:33, 1, :], start=False, stop=True), "sT")
                sc_last = sc_stop
                a0 = cfg.GA + tc * 128 - sb_i * cfg.n_mem * TT
                Gv = G[:, a0:a0 - SBLK:-1] if a0 - SBLK >= 0 else G[:, a0::-1]
                p.wait("vector", "sT", sc_stop)
                mask_d = p.inc("vector", lambda e, pb=pb, Gv=Gv: e.tensor_add(pb, pb, Gv), "sV")
                p.wait("scalar", "sV", mask_d)
                if tc >= 4:
                    p.wait("scalar", "sT", pv_done[(hg, tc - 4)])
                exp_done[tc] = p.inc("scalar", lambda e, tc=tc, pb=pb: e.activation(
                    probs[:, tc % 4, :], pb, AF.Exp, scale=float(cfg.scale)), "sS")
                if tc >= 1:
                    emit_pv(tc - 1)
            emit_pv(n_run - 1)
            prev_exp_last = exp_done[n_run - 1]
            def make_tail(hg=hg, h=h, hp=hp, w_v=w_v):
                def tail():
                    nonlocal v_iden_prev
                    p.wait("vector", "sT", pv_done[(hg, cfg.LSB[hg // n_heads] - 1)])
                    v_iden = p.inc("vector", lambda e: e.reciprocal(iden[0:1, 0:SBLK], ps[0:1, 2 * 512:2 * 512 + SBLK]), "sV")
                    v_iden_prev = v_iden
                    p.wait("tensor", "sV", v_iden)
                    if hg >= 1:
                        p.wait("tensor", "sS", at_ev[hg - 1])
                    t_bc = p.inc("tensor", lambda e: e.matmul(
                        ps[:, 7 * 512:7 * 512 + SBLK], ones1_f32[:, :], iden[0:1, 0:SBLK], start=True, stop=True), "sT")
                    p.wait("vector", "sT", t_bc)
                    p.inc("vector", lambda e: e.tensor_copy(invb[:, :], ps[:, 7 * 512:7 * 512 + SBLK]), "sV")
                    ol_ev = [None] * n_rc
                    for rc in range(n_rc):
                        ol_ev[rc] = p.inc("vector", lambda e, rc=rc: e.tensor_mul(
                            olb[:, rc, :], ps[:, (3 + rc) * 512:(3 + rc) * 512 + SBLK], invb[:, :]), "sV")
                    p.wait("tensor", "sV", ol_ev[n_rc - 1])
                    p.wait("tensor", "sW", w_v)
                    for rc in range(n_rc):
                        fn = lambda e, rc=rc, hp=hp: e.matmul(
                            ps[:, 7 * 512:7 * 512 + SBLK], vbuf[:, hp, rc, :], olb[:, rc, :],
                            start=(rc == 0), stop=(rc == n_rc - 1))
                        if rc == n_rc - 1:
                            hw[hg] = p.inc("tensor", fn, "sT")
                        else:
                            p.emit("tensor", fn)
                    p.wait("scalar", "sT", hw[hg])
                    at_ev[hg] = p.inc("scalar", lambda e, h=h: e.activation(
                        attn[:, h, :], ps[:, 7 * 512:7 * 512 + SBLK], AF.Copy), "sS")
                return tail
            pending_tail = make_tail()
        pending_tail()
        pending_tail = None
        p.wait("tensor", "sS", at_ev[sb_i * n_heads + n_heads - 1])
        for oc in range(n_dc):
            par = oc % 2
            goc = sb_i * n_dc + oc
            if (sb_i, oc - 2) in wo:
                p.wait("gpsimd", "sT", wo[(sb_i, oc - 2)])
            elif oc < 2:
                p.wait("gpsimd", "sT", hw[sb_i * n_heads + n_heads - 1])
            w_o = p.inc("gpsimd", lambda e, oc=oc, par=par: e.dma_start(obuf[:, par, :, :], t["wout_t"][oc]), "sW", 16)
            if goc - 2 in wo_ev:
                p.wait("sync", "sV", wo_ev[goc - 2])
            a_r = p.inc("sync", lambda e, sb_i=sb_i, oc=oc: e.dma_start(
                xres[:, oc % 2, :], t["x1_fm"][sb_i][:, oc, :]), "sA", 16)
            p.wait("tensor", "sW", w_o)
            pb = ps[:, par * 512:par * 512 + SBLK]
            if goc - 2 in wo_ev:
                p.wait("tensor", "sV", wo_ev[goc - 2])
            for hc in range(n_heads):
                fn = lambda e, oc=oc, hc=hc, par=par, pb=pb: e.matmul(
                    pb, obuf[:, par, hc, :], attn[:, hc, :], start=(hc == 0), stop=(hc == n_heads - 1))
                if hc == n_heads - 1:
                    wo[(sb_i, oc)] = p.inc("tensor", fn, "sT")
                else:
                    p.emit("tensor", fn)
            p.wait("vector", "sT", wo[(sb_i, oc)])
            p.wait("vector", "sA", a_r)
            if goc - 2 in wo_st:
                p.wait("vector", "sA", wo_st[goc - 2])
            wo_ev[goc] = p.inc("vector", lambda e, oc=oc, pb=pb: e.tensor_add(
                x2t[:, oc % 2, :], pb, xres[:, oc % 2, :]), "sV")
            p.wait("sync", "sV", wo_ev[goc])
            wo_st[goc] = p.inc("sync", lambda e, sb_i=sb_i, oc=oc: e.dma_start(
                t["x2_fm"][sb_i][:, oc, :], x2t[:, oc % 2, :]), "sA", 16)


def build_program(cfg: Cfg):
    nc = bass.Bass("TRN2", target_bir_lowering=False)
    p = Prog(nc)
    for s in ("sW", "sA", "sT", "sV", "sS", "sC1", "sC2"):
        p.sem(s)
    for e in ENGINES:
        p.sem("bar_" + e)

    n_dc, n_ic, n_qc, n_rc, n_heads = cfg.n_dc, cfg.n_ic, cfg.n_qc, cfg.n_rc, cfg.n_heads
    TT, NT, T = cfg.TT, cfg.NT, cfg.T
    CK = n_rc * 128 + 64

    t = {}
    def ein(name, shape, dtype):
        t[name] = nc.dram_tensor(name, shape, dtype, kind="ExternalInput").ap()
    ein("x_fm", [NT, 128, n_dc, TT], F32)
    ein("guw_t", [n_ic, 128, 2, n_dc, 128], I32)
    ein("down_t", [cfg.n_hc, 128, n_ic, 128], I32)
    ein("gs_us_h", [128, n_ic], F32)
    ein("ds_half_h", [128, cfg.n_hc], F32)
    ein("wqd_t", [n_qc, 128, n_dc, 128], F32)
    ein("wqun_t", [n_heads, 128, n_qc, 128], F32)
    ein("wqur_t", [n_heads, 128, n_qc, 64], F32)
    ein("wkv_fm", [128, n_dc, n_rc * 128], F32)
    ein("wkr_fm", [128, n_dc, 64], F32)
    ein("wuk_fm", [n_heads, 128, n_rc * 128], F32)
    ein("wuv_tt", [n_heads, 128, n_rc, 128], F32)
    ein("wout_t", [n_dc, 128, n_heads, 128], F32)
    ein("cos_q_h", [32, T], F32)
    ein("sin_q_h", [32, T], F32)
    ein("G_h", [128, cfg.GLEN], F32)
    t["y_fm"] = nc.dram_tensor("y_fm", [NT, 128, n_dc, TT], F32, kind="ExternalOutput").ap()
    t["xln1"] = nc.dram_tensor("xln1", [NT, 128, n_dc, TT], BF16).ap()
    t["xln2"] = nc.dram_tensor("xln2", [NT, 128, n_dc, TT], BF16).ap()
    t["xln3"] = nc.dram_tensor("xln3", [NT, 128, n_dc, TT], BF16).ap()
    t["x1_fm"] = nc.dram_tensor("x1_fm", [NT, 128, n_dc, TT], F32).ap()
    t["x2_fm"] = nc.dram_tensor("x2_fm", [NT, 128, n_dc, TT], F32).ap()
    import os
    if os.environ.get("MK_DEBUG"):
        for tg in ("_f1", "_f2"):
            t["dbg_gu" + tg] = nc.dram_tensor("dbg_gu" + tg, [NT, 128, n_ic, TT], BF16).ap()
            t["dbg_xb" + tg] = nc.dram_tensor("dbg_xb" + tg, [NT, 128, n_dc, TT], BF16).ap()
            t["dbg_xb2" + tg] = nc.dram_tensor("dbg_xb2" + tg, [NT, 128, n_dc, TT], BF16).ap()
            t["dbg_db" + tg] = nc.dram_tensor("dbg_db" + tg, [128, (n_ic + 3) // 4, 128], BF16).ap()
            t["dbg_y" + tg] = nc.dram_tensor("dbg_y" + tg, [NT, 128, n_dc, TT], F32).ap()
    t["kv_contrib"] = nc.dram_tensor("kv_contrib", [CK, T], BF16).ap()
    t["tm_contrib"] = nc.dram_tensor("tm_contrib", [T, n_rc * 128], BF16).ap()
    t["kv_gath"] = nc.dram_tensor("kv_gath", [cfg.n_mem * CK, T], BF16)
    t["tm_gath"] = nc.dram_tensor("tm_gath", [cfg.n_mem * T, n_rc * 128], BF16)

    psu = nc.alloc_psum_tensor("PSU", [128, 4096], F32).ap()

    sb = {}
    guards = {"persist": [], "ln": [], "ffn": [], "kv": [], "at": []}

    nname = [0]
    def ph_alloc(group, name, shape, dtype):
        nname[0] += 1
        g = nc.sbuf_tensor(f"{name}__{nname[0]}", shape, dtype)
        h = g.__enter__()
        guards[group].append(g)
        sb[name] = h.ap()
        return h

    def ph_free(group):
        for g in reversed(guards[group]):
            g.__exit__(None, None, None)
        guards[group] = []

    ph_alloc("persist", "ones_bf", [128, 1], BF16)
    ph_alloc("persist", "ones1_f32", [1, 128], F32)
    ph_alloc("persist", "gs_us", [128, n_ic], F32)
    ph_alloc("persist", "ds_half", [128, cfg.n_hc], F32)
    ph_alloc("persist", "cos_q", [32, T], F32)
    ph_alloc("persist", "sin_q", [32, T], F32)
    ph_alloc("persist", "mk2", [1, 1], F32)
    ph_alloc("persist", "cst", [1, 2], F32)

    p.emit("vector", lambda e: e.memset(sb["ones_bf"][:, :], 1.0))
    p.emit("vector", lambda e: e.memset(sb["cst"][0:1, 0:1], float(cfg.eps)))
    p.emit("vector", lambda e: e.memset(sb["cst"][0:1, 1:2], 0.02))
    p.inc("vector", lambda e: e.memset(sb["ones1_f32"][:, :], 1.0), "sV")
    p.inc("sync", lambda e: e.dma_start(sb["gs_us"][:, :], t["gs_us_h"][:, :]), "sA", 16)
    p.inc("sync", lambda e: e.dma_start(sb["ds_half"][:, :], t["ds_half_h"][:, :]), "sA", 16)
    p.inc("sync", lambda e: e.dma_start(sb["cos_q"][:, :], t["cos_q_h"][:, :]), "sA", 16)
    p.inc("sync", lambda e: e.dma_start(sb["sin_q"][:, :], t["sin_q_h"][:, :]), "sA", 16)
    p.barrier()

    def alloc_ln():
        ph_alloc("ln", "ln_xf", [128, n_dc, TT], F32)
        ph_alloc("ln", "ln_xb", [128, n_dc, TT], BF16)
        ph_alloc("ln", "ln_xsq", [128, n_dc, TT], BF16)
        ph_alloc("ln", "ln_ab", [128, TT], BF16)
        ph_alloc("ln", "ln_bb", [128, TT], BF16)
        ph_alloc("ln", "ln_rows", [1, 6 * TT], F32)

    HDC = n_dc // 2
    QI = (n_ic + 3) // 4
    def alloc_ffn():
        ph_alloc("ffn", "ffn_xb", [128, 2, n_dc, TT], BF16)
        ph_alloc("ffn", "ffn_guw", [128, 3, 2, HDC, 128], BF16)
        ph_alloc("ffn", "ffn_dbuf", [128, 2, QI, 128], BF16)
        ph_alloc("ffn", "ffn_gu", [128, n_ic, TT], BF16)
        ph_alloc("ffn", "ffn_gsc", [128, 2, TT], BF16)
        ph_alloc("ffn", "ffn_xres", [128, 2, TT], F32)
        ph_alloc("ffn", "ffn_x1t", [128, 2, TT], F32)
        ph_alloc("ffn", "ffn_dbgy", [128, TT], F32)

    alloc_ln()
    emit_layernorm(p, cfg, t["x_fm"], t["xln1"], sb, psu)
    p.barrier()
    ph_free("ln")
    alloc_ffn()
    emit_ffn(p, cfg, t, sb, psu, "xln1", "x_fm", "x1_fm", tag="_f1")
    p.barrier()
    ph_free("ffn")
    alloc_ln()
    emit_layernorm(p, cfg, t["x1_fm"], t["xln2"], sb, psu)
    p.barrier()
    ph_free("ln")

    ph_alloc("kv", "kv_xb", [128, n_dc, TT], BF16)
    ph_alloc("kv", "kv_wkv", [128, n_dc, n_rc * 128], BF16)
    ph_alloc("kv", "kv_wkr", [128, n_dc, 64], BF16)
    ph_alloc("kv", "kv_cvt", [128, TT], BF16)
    ph_alloc("kv", "kv_tmt", [128, n_rc * 128], BF16)
    ph_alloc("kv", "kv_krt", [32, 2, T], BF16)
    ph_alloc("kv", "kv_rtmp", [32, 2, TT], F32)
    emit_kv(p, cfg, t, sb, psu)
    p.barrier()
    ph_free("kv")

    R = ph_alloc("at", "at_R", [128, n_dc * TT], BF16)
    sb["at_xbq"] = R.ap().rearrange("p (a b) -> p a b", a=n_dc)
    sb["at_attn"] = R.ap().rearrange("p (a b) -> p a b", a=n_heads)
    wcols = max(2 * max(n_heads, n_dc) * 128,
                2 * n_qc * 128 + 2 * n_qc * 64 + 4 * n_rc * 128)
    W = ph_alloc("at", "at_W", [128, wcols], BF16)
    sb["at_qdbuf"] = W.ap()[:, 0:2 * n_dc * 128].rearrange("p (a b c) -> p a b c", a=2, b=n_dc)
    sb["at_obuf"] = W.ap()[:, 0:2 * n_heads * 128].rearrange("p (a b c) -> p a b c", a=2, b=n_heads)
    hwap = W.ap()
    o1 = 2 * n_qc * 128
    o2 = o1 + 2 * n_qc * 64
    o3 = o2 + 2 * n_rc * 128
    sb["at_nbuf"] = hwap[:, 0:o1].rearrange("p (a b c) -> p a b c", a=2, b=n_qc)
    sb["at_rbuf"] = hwap[:, o1:o2].rearrange("p (a b c) -> p a b c", a=2, b=n_qc)
    sb["at_ukbuf"] = hwap[:, o2:o3].rearrange("p (a b) -> p a b", a=2)
    sb["at_vbuf"] = hwap[:, o3:o3 + 2 * n_rc * 128].rearrange("p (a b c) -> p a b c", a=2, b=n_rc)
    ph_alloc("at", "at_ckv", [128, n_rc, cfg.n_mem * T], BF16)
    ph_alloc("at", "at_tm", [128, cfg.n_tc, n_rc * 128], BF16)
    ph_alloc("at", "at_kr", [33, 2, cfg.n_mem * T], BF16)
    ph_alloc("at", "G", [128, cfg.GLEN], BF16)
    ph_alloc("at", "at_sq", [128, 3, 512], BF16)
    ph_alloc("at", "at_nrow", [1, 2], F32)
    ph_alloc("at", "at_cq", [128, n_qc, cfg.SBLK], BF16)
    ph_alloc("at", "at_qn", [128, cfg.SBLK], BF16)
    ph_alloc("at", "at_qlat", [128, n_rc, cfg.SBLK], BF16)
    ph_alloc("at", "at_qrope", [33, 2, cfg.SBLK], BF16)
    ph_alloc("at", "at_brow", [1, cfg.SBLK], BF16)
    ph_alloc("at", "at_probs", [128, 4, cfg.SBLK], BF16)
    ph_alloc("at", "at_olb", [128, n_rc, cfg.SBLK], BF16)
    ph_alloc("at", "at_invb", [128, cfg.SBLK], F32)
    ph_alloc("at", "at_iden", [1, 2 * cfg.SBLK], F32)
    ph_alloc("at", "at_xres", [128, 2, TT], F32)
    ph_alloc("at", "at_x2t", [128, 2, TT], F32)
    ph_alloc("at", "kv_rtmp", [32, 2, TT], F32)
    emit_attention(p, cfg, t, sb, psu)
    p.barrier()
    ph_free("at")

    alloc_ln()
    emit_layernorm(p, cfg, t["x2_fm"], t["xln3"], sb, psu)
    p.barrier()
    ph_free("ln")
    alloc_ffn()
    emit_ffn(p, cfg, t, sb, psu, "xln3", "x2_fm", "y_fm", tag="_f2")
    p.wait("sync", "sA", p.cnt["sA"])
    p.barrier()

    p.replay()
    p.close()
    return nc


def yarn_tables(cfg, positions):
    DR = cfg.dr
    freqs = 1.0 / 10000.0 ** (np.arange(0, DR, 2, dtype=np.float64) / DR)
    wl = 2.0 * np.pi / freqs
    ramp = np.clip((wl / 4096.0 - 1.0) / (32.0 - 1.0), 0.0, 1.0)
    scale = 1.0 - ramp + ramp * 40.0
    inv = freqs / scale
    f = positions[None, :].astype(np.float64) * inv[:, None]
    return np.cos(f).astype(np.float32), np.sin(f).astype(np.float32)


def core_positions(cfg, core):
    mi = core % cfg.n_mem
    return [np.arange(ts * cfg.n_mem * cfg.TT + mi * cfg.TT,
                      ts * cfg.n_mem * cfg.TT + (mi + 1) * cfg.TT) for ts in range(cfg.NT)]


def host_prep(cfg: Cfg, inputs, core):
    n_dc, n_ic, n_qc, n_rc, n_heads = cfg.n_dc, cfg.n_ic, cfg.n_qc, cfg.n_rc, cfg.n_heads
    TT, NT, T = cfg.TT, cfg.NT, cfg.T
    b = core // cfg.n_mem
    posl = core_positions(cfg, core)
    x = inputs["L0_attn_hidden_states"]
    tok = np.concatenate([np.asarray(x[b, p[0]:p[0] + TT, :]) for p in posl], axis=0)
    m = {}
    m["x_fm"] = np.ascontiguousarray(tok.reshape(NT, TT, n_dc, 128).transpose(0, 3, 2, 1))
    gate = inputs["L0_ffn_gate_wq"]; up = inputs["L0_ffn_up_wq"]; down = inputs["L0_ffn_down_wq"]
    gt = gate.reshape(n_ic, 128, n_dc, 128)
    ut = up.reshape(n_ic, 128, n_dc, 128)
    guw = np.stack([gt, ut], axis=1)
    m["guw_t"] = np.ascontiguousarray(guw.transpose(0, 4, 1, 3, 2))
    dt_ = down.reshape(cfg.n_hc, 128, n_ic, 128)
    m["down_t"] = np.ascontiguousarray(dt_.transpose(0, 3, 2, 1))
    gs = inputs["L0_ffn_gate_scale"]; us = inputs["L0_ffn_up_scale"]; ds = inputs["L0_ffn_down_scale"]
    m["gs_us_h"] = np.ascontiguousarray((gs * us).reshape(n_ic, 128).T)
    m["ds_half_h"] = np.ascontiguousarray((0.5 * ds).reshape(cfg.n_hc, 128).T)
    wqd = inputs["L0_attn_w_q_down"]
    m["wqd_t"] = np.ascontiguousarray(wqd.reshape(n_dc, 128, n_qc, 128).transpose(2, 1, 0, 3))
    wqun = inputs["L0_attn_w_q_up_nope"]
    m["wqun_t"] = np.ascontiguousarray(wqun.reshape(n_qc, 128, n_heads, 128).transpose(2, 1, 0, 3))
    wqur = inputs["L0_attn_w_q_up_rope"]
    m["wqur_t"] = np.ascontiguousarray(wqur.reshape(n_qc, 128, n_heads, cfg.dr).transpose(2, 1, 0, 3))
    wkv = inputs["L0_attn_w_kv_down"]
    m["wkv_fm"] = np.ascontiguousarray(wkv.reshape(n_dc, 128, n_rc * 128).transpose(1, 0, 2))
    wkr = inputs["L0_attn_w_k_rope"]
    m["wkr_fm"] = np.ascontiguousarray(wkr.reshape(n_dc, 128, cfg.dr).transpose(1, 0, 2))
    wuk = inputs["L0_attn_w_uk"]
    m["wuk_fm"] = np.ascontiguousarray(wuk.reshape(n_heads, 128, n_rc * 128))
    wuv = inputs["L0_attn_w_uv"]
    m["wuv_tt"] = np.ascontiguousarray(wuv.reshape(n_heads, 128, n_rc, 128).transpose(0, 3, 2, 1))
    wout = inputs["L0_attn_w_out"]
    m["wout_t"] = np.ascontiguousarray(wout.reshape(n_heads, 128, n_dc, 128).transpose(2, 1, 0, 3))
    pos = np.concatenate(posl)
    c_, s_ = yarn_tables(cfg, pos)
    m["cos_q_h"] = np.ascontiguousarray(c_)
    m["sin_q_h"] = np.ascontiguousarray(s_)
    thresh = cfg.GA + (core % cfg.n_mem) * TT
    pk = np.arange(128)[:, None] + np.arange(cfg.GLEN)[None, :]
    m["G_h"] = np.where(pk <= thresh, 0.0, -30000.0).astype(np.float32)
    return m


def host_unprep(cfg, y_fm):
    return np.ascontiguousarray(y_fm.transpose(0, 3, 2, 1).reshape(cfg.T, cfg.D))


from concourse.bass_utils import run_bass_kernel_spmd


def _host_shared(cfg, inputs):
    n_dc, n_ic, n_qc, n_rc, n_heads = cfg.n_dc, cfg.n_ic, cfg.n_qc, cfg.n_rc, cfg.n_heads
    m = {}
    gate = inputs["L0_ffn_gate_wq"]; up = inputs["L0_ffn_up_wq"]; down = inputs["L0_ffn_down_wq"]
    gt = gate.reshape(n_ic, 128, n_dc, 128)
    ut = up.reshape(n_ic, 128, n_dc, 128)
    guw = np.stack([gt, ut], axis=1)
    m["guw_t"] = np.ascontiguousarray(guw.transpose(0, 4, 1, 3, 2))
    dt_ = down.reshape(cfg.n_hc, 128, n_ic, 128)
    m["down_t"] = np.ascontiguousarray(dt_.transpose(0, 3, 2, 1))
    gs = inputs["L0_ffn_gate_scale"]; us = inputs["L0_ffn_up_scale"]; ds = inputs["L0_ffn_down_scale"]
    m["gs_us_h"] = np.ascontiguousarray((gs * us).reshape(n_ic, 128).T)
    m["ds_half_h"] = np.ascontiguousarray((0.5 * ds).reshape(cfg.n_hc, 128).T)
    m["wqd_t"] = np.ascontiguousarray(inputs["L0_attn_w_q_down"].reshape(n_dc, 128, n_qc, 128).transpose(2, 1, 0, 3))
    m["wqun_t"] = np.ascontiguousarray(inputs["L0_attn_w_q_up_nope"].reshape(n_qc, 128, n_heads, 128).transpose(2, 1, 0, 3))
    m["wqur_t"] = np.ascontiguousarray(inputs["L0_attn_w_q_up_rope"].reshape(n_qc, 128, n_heads, cfg.dr).transpose(2, 1, 0, 3))
    m["wkv_fm"] = np.ascontiguousarray(inputs["L0_attn_w_kv_down"].reshape(n_dc, 128, n_rc * 128).transpose(1, 0, 2))
    m["wkr_fm"] = np.ascontiguousarray(inputs["L0_attn_w_k_rope"].reshape(n_dc, 128, cfg.dr).transpose(1, 0, 2))
    m["wuk_fm"] = np.ascontiguousarray(inputs["L0_attn_w_uk"].reshape(n_heads, 128, n_rc * 128))
    m["wuv_tt"] = np.ascontiguousarray(inputs["L0_attn_w_uv"].reshape(n_heads, 128, n_rc, 128).transpose(0, 3, 2, 1))
    m["wout_t"] = np.ascontiguousarray(inputs["L0_attn_w_out"].reshape(n_heads, 128, n_dc, 128).transpose(2, 1, 0, 3))
    return m


def _host_core(cfg, inputs, core):
    n_dc, TT, NT, T = cfg.n_dc, cfg.TT, cfg.NT, cfg.T
    b = core // cfg.n_mem
    posl = core_positions(cfg, core)
    x = inputs["L0_attn_hidden_states"]
    tok = np.concatenate([np.asarray(x[b, p[0]:p[0] + TT, :]) for p in posl], axis=0)
    m = {}
    m["x_fm"] = np.ascontiguousarray(tok.reshape(NT, TT, n_dc, 128).transpose(0, 3, 2, 1))
    pos = np.concatenate(posl)
    c_, s_ = yarn_tables(cfg, pos)
    m["cos_q_h"] = np.ascontiguousarray(c_)
    m["sin_q_h"] = np.ascontiguousarray(s_)
    thresh = cfg.GA + (core % cfg.n_mem) * TT
    pk = np.arange(128)[:, None] + np.arange(cfg.GLEN)[None, :]
    m["G_h"] = np.where(pk <= thresh, 0.0, -30000.0).astype(np.float32)
    return m


def kernel(**inputs):
    cfg = Cfg()
    nc = build_program(cfg)
    shared = _host_shared(cfg, inputs)
    in_maps = []
    for c in range(8):
        m = dict(shared)
        m.update(_host_core(cfg, inputs, c))
        in_maps.append(m)
    res = run_bass_kernel_spmd(nc, in_maps, core_ids=list(range(8)))
    out = np.empty((2, 4096, 4096), dtype=np.float32)
    for c in range(8):
        y = np.asarray(res.results[c]["y_fm"]).reshape(cfg.NT, 128, cfg.n_dc, cfg.TT)
        b = c // cfg.n_mem
        rows = host_unprep(cfg, y)
        posl = core_positions(cfg, c)
        for ts in range(cfg.NT):
            o = int(posl[ts][0])
            out[b, o:o + cfg.TT, :] = rows[ts * cfg.TT:(ts + 1) * cfg.TT]
    return out


# version 09
# speedup vs baseline: 1.0883x; 1.0883x over previous
import numpy as np
import ml_dtypes
import concourse.bass as bass
import concourse.mybir as mybir

F32, BF16, I32 = mybir.dt.float32, mybir.dt.bfloat16, mybir.dt.int32
AF = mybir.ActivationFunctionType
ALU = mybir.AluOpType
ENGINES = ("tensor", "vector", "scalar", "sync", "gpsimd")


class Cfg:
    def __init__(self, n_dc=32, n_ic=86, n_qc=12, n_heads=32, n_rc=4, TT=512, NT=2,
                 n_mem=4, dr=64, eps=1e-5):
        self.n_dc, self.n_ic, self.n_qc = n_dc, n_ic, n_qc
        self.n_heads, self.n_rc = n_heads, n_rc
        self.TT, self.NT, self.n_mem, self.dr = TT, NT, n_mem, dr
        self.eps = eps
        self.D = n_dc * 128
        self.I = n_ic * 128
        self.QL = n_qc * 128
        self.KVL = n_rc * 128
        self.T = TT * NT
        self.S = n_mem * self.T
        self.n_tc = self.S // 128
        self.n_hc = n_dc
        self.SBLK = TT
        self.scale = 1.0 / np.sqrt(128.0 + dr)
        self.GA = self.S - 1
        self.GLEN = self.GA + (self.n_tc - 1) * 128 + 1
        self.LSB = [((sb + 1) * n_mem * TT) // 128 for sb in range(NT)]


class Prog:
    def __init__(self, nc):
        self.nc = nc
        self.ops = {e: [] for e in ENGINES}
        self.sems = {}
        self.cnt = {}
        self._guards = []

    def sem(self, name):
        g = self.nc.semaphore(name)
        h = g.__enter__()
        self._guards.append(g)
        self.sems[name] = h
        self.cnt[name] = 0
        return h

    def emit(self, eng, fn):
        self.ops[eng].append(fn)

    def inc(self, eng, fn, sem, n=1):
        h = self.sems[sem]
        self.emit(eng, lambda e, fn=fn, h=h, n=n: fn(e).then_inc(h, n))
        self.cnt[sem] += n
        return self.cnt[sem]

    def wait(self, eng, sem, val):
        if val is None:
            return
        h = self.sems[sem]
        self.emit(eng, lambda e, h=h, val=val: e.wait_ge(h, val))

    def barrier(self):
        self.wait("sync", "sA", self.cnt["sA"])
        self.wait("gpsimd", "sW", self.cnt["sW"])
        vals = {}
        for eng in ENGINES:
            s = "bar_" + eng
            h = self.sems[s]
            self.emit(eng, lambda e, h=h: e.nop().then_inc(h, 1))
            self.cnt[s] += 1
            vals[eng] = self.cnt[s]
        for eng in ENGINES:
            for other in ENGINES:
                if other != eng:
                    self.wait(eng, "bar_" + other, vals[other])

    def replay(self):
        with self.nc.Block() as block:
            block.sync(lambda e: [fn(e) for fn in self.ops["sync"]])
            block.gpsimd(lambda e: [fn(e) for fn in self.ops["gpsimd"]])
            block.tensor(lambda e: [fn(e) for fn in self.ops["tensor"]])
            block.vector(lambda e: [fn(e) for fn in self.ops["vector"]])
            block.scalar(lambda e: [fn(e) for fn in self.ops["scalar"]])

    def close(self):
        for g in reversed(self._guards):
            g.__exit__(None, None, None)


def emit_layernorm(p, cfg, src_ap, dst_ap, sb, ps):
    n_dc, TT, NT = cfg.n_dc, cfg.TT, cfg.NT
    xf, xb, xsq = sb["ln_xf"], sb["ln_xb"], sb["ln_xsq"]
    ab, bb, rowm = sb["ln_ab"], sb["ln_bb"], sb["ln_rows"]
    ones_bf, ones1_f32 = sb["ones_bf"], sb["ones1_f32"]
    NQ = 4
    DQ = n_dc // NQ
    v_out = None
    for ts in range(NT):
        lds, casts, sqs = [], [], []
        for q in range(NQ):
            if v_out is not None and q == 0:
                p.wait("sync", "sV", v_out)
            lds.append(p.inc("sync", lambda e, s=src_ap, ts=ts, q=q: e.dma_start(
                xf[:, q * DQ:(q + 1) * DQ, :], s[ts][:, q * DQ:(q + 1) * DQ, :]), "sA", 16))
        stA = ps[0:1, 6 * 512:6 * 512 + TT]
        stB = ps[0:1, 7 * 512:7 * 512 + TT]
        for q in range(NQ):
            p.wait("vector", "sA", lds[q])
            casts.append(p.inc("vector", lambda e, q=q: e.tensor_copy(
                xb[:, q * DQ:(q + 1) * DQ, :], xf[:, q * DQ:(q + 1) * DQ, :]), "sV"))
            p.wait("scalar", "sA", lds[q])
            sqs.append(p.inc("scalar", lambda e, q=q: e.activation(
                xsq[:, q * DQ:(q + 1) * DQ, :], xf[:, q * DQ:(q + 1) * DQ, :], AF.Square), "sS"))
            p.wait("tensor", "sV", casts[q])
            p.wait("tensor", "sS", sqs[q])
            for d in range(DQ):
                dc = q * DQ + d
                p.emit("tensor", lambda e, dc=dc, stA=stA: e.matmul(
                    stA, ones_bf[:, :], xb[:, dc, :], start=(dc == 0), stop=(dc == n_dc - 1)))
            for d in range(DQ):
                dc = q * DQ + d
                fn = lambda e, dc=dc, stB=stB: e.matmul(
                    stB, ones_bf[:, :], xsq[:, dc, :], start=(dc == 0), stop=(dc == n_dc - 1))
                if dc == n_dc - 1:
                    t_st = p.inc("tensor", fn, "sT")
                else:
                    p.emit("tensor", fn)
        mean = rowm[0:1, 0:TT]
        e2 = rowm[0:1, TT:2 * TT]
        var = rowm[0:1, 2 * TT:3 * TT]
        lnv = rowm[0:1, 3 * TT:4 * TT]
        rstd = rowm[0:1, 4 * TT:5 * TT]
        mb = rowm[0:1, 5 * TT:6 * TT]
        p.wait("vector", "sT", t_st)
        p.emit("vector", lambda e, stA=stA: e.tensor_scalar_mul(mean, stA, 1.0 / cfg.D))
        p.emit("vector", lambda e, stB=stB: e.tensor_scalar_mul(e2, stB, 1.0 / cfg.D))
        p.emit("vector", lambda e: e.tensor_mul(var, mean, mean))
        v_var = p.inc("vector", lambda e: e.tensor_sub(var, e2, var), "sV")
        p.wait("scalar", "sV", v_var)
        p.emit("scalar", lambda e: e.activation(lnv, var, AF.Ln, bias=sb["cst"][0:1, 0:1]))
        s_rstd = p.inc("scalar", lambda e: e.activation(rstd, lnv, AF.Exp, scale=-0.5), "sS")
        p.wait("vector", "sS", s_rstd)
        p.emit("vector", lambda e: e.tensor_mul(mb, mean, rstd))
        v_mb = p.inc("vector", lambda e: e.tensor_scalar_mul(mb, mb, -1.0), "sV")
        p.wait("tensor", "sV", v_mb)
        bcA = ps[:, 4 * 512:4 * 512 + TT]
        bcB = ps[:, 5 * 512:5 * 512 + TT]
        p.emit("tensor", lambda e, bcA=bcA: e.matmul(bcA, ones1_f32[:, :], rstd, start=True, stop=True))
        t_bc = p.inc("tensor", lambda e, bcB=bcB: e.matmul(bcB, ones1_f32[:, :], mb, start=True, stop=True), "sT")
        p.wait("vector", "sT", t_bc)
        p.emit("vector", lambda e, bcA=bcA: e.tensor_copy(ab[:, :], bcA))
        p.inc("vector", lambda e, bcB=bcB: e.tensor_copy(bb[:, :], bcB), "sV")
        for q in range(NQ):
            for d in range(DQ):
                dc = q * DQ + d
                p.emit("vector", lambda e, dc=dc: e.tensor_mul(
                    xb[:, dc, :], xb[:, dc, :], ab[:, :]))
                fn = lambda e, dc=dc: e.tensor_add(xb[:, dc, :], xb[:, dc, :], bb[:, :])
                if d == DQ - 1:
                    v_q = p.inc("vector", fn, "sV")
                else:
                    p.emit("vector", fn)
            if q == NQ - 1:
                v_out = v_q
            p.wait("sync", "sV", v_q)
            p.inc("sync", lambda e, d=dst_ap, ts=ts, q=q: e.dma_start(
                d[ts][:, q * DQ:(q + 1) * DQ, :], xb[:, q * DQ:(q + 1) * DQ, :]), "sA", 16)


def emit_ffn(p, cfg, t, sb, ps, src_name, res_name, dst_name, tag="", use_bf=False):
    n_dc, n_ic, n_hc, TT, NT = cfg.n_dc, cfg.n_ic, cfg.n_hc, cfg.TT, cfg.NT
    xb, guw, dbuf = sb["ffn_xb"], sb["ffn_guw"], sb["ffn_dbuf"]
    gu, gsc, xres, x1t = sb["ffn_gu"], sb["ffn_gsc"], sb["ffn_xres"], sb["ffn_x1t"]
    gs_us, ds_half = sb["gs_us"], sb["ds_half"]
    src, res, dst = t[src_name], t[res_name], t[dst_name]
    HDC = n_dc // 2
    NGW = 3
    QI = (n_ic + 3) // 4
    NQ = (n_ic + QI - 1) // QI
    gfree = {}
    g_evac, u_used = {}, {}
    dcons = {}
    ev_done, st_done = {}, {}

    def emit_gload(k):
        gts, rem = divmod(k, 2 * n_ic)
        ic, half = rem // 2, rem % 2
        if k - NGW >= 0:
            p.wait("gpsimd", "sT", gfree[k - NGW])
        wsrc = t["guw_bf"] if use_bf else t["guw_t"]
        return p.inc("gpsimd", lambda e, ic=ic, half=half, k=k, wsrc=wsrc: e.dma_start(
            guw[:, k % NGW, :, :, :], wsrc[ic, :, :, half * HDC:(half + 1) * HDC, :]), "sW", 16)

    def emit_dload(k):
        dts, rem = divmod(k, n_hc * NQ)
        hc, q = rem // NQ, rem % NQ
        lo = q * QI
        n_t = min(QI, n_ic - lo)
        assert n_t > 0
        if k - 2 >= 0:
            p.wait("gpsimd", "sT", dcons[k - 2])
        dsrc = t["down_bf"] if use_bf else t["down_t"]
        return p.inc("gpsimd", lambda e, hc=hc, lo=lo, n_t=n_t, k=k, dsrc=dsrc: e.dma_start(
            dbuf[:, k % 2, 0:n_t, :], dsrc[hc][:, lo:lo + n_t, :]), "sW", 16)

    gready, dready = {}, {}
    for ts in range(NT):
        if ts > 1:
            p.wait("sync", "sT", gfree[2 * ((ts - 1) * n_ic) - 1])
        a_x = p.inc("sync", lambda e, ts=ts: e.dma_start(xb[:, ts % 2, :, :], src[ts]), "sA", 16)
        if "dbg_xb2" + tag in t:
            p.wait("sync", "sA", a_x)
            a_x = p.inc("sync", lambda e, ts=ts: e.dma_start(t["dbg_xb2" + tag][ts], xb[:, ts % 2, :, :]), "sA", 16)
        for ic in range(n_ic):
            gic = ts * n_ic + ic
            for half in range(2):
                gready[2 * gic + half] = emit_gload(2 * gic + half)
            if ic == n_ic - 3:
                bk = ts * n_hc * NQ
                dready[bk] = emit_dload(bk)
                dready[bk + 1] = emit_dload(bk + 1)
            par = ic % 2
            pg = ps[:, par * 512:par * 512 + TT]
            pu = ps[:, (2 + par) * 512:(2 + par) * 512 + TT]
            if gic - 2 >= 0:
                p.wait("tensor", "sS", g_evac[gic - 2])
                p.wait("tensor", "sV", u_used[gic - 2])
            if ic == 0:
                p.wait("tensor", "sA", a_x)
            for half in range(2):
                k = 2 * gic + half
                gsl = k % NGW
                p.wait("tensor", "sW", gready[k])
                for d in range(HDC):
                    dc = half * HDC + d
                    p.emit("tensor", lambda e, gsl=gsl, d=d, dc=dc, pg=pg, ts=ts: e.matmul(
                        pg, guw[:, gsl, 0, d, :], xb[:, ts % 2, dc, :], start=(dc == 0), stop=(dc == n_dc - 1)))
                for d in range(HDC):
                    dc = half * HDC + d
                    fnu = lambda e, gsl=gsl, d=d, dc=dc, pu=pu, ts=ts: e.matmul(
                        pu, guw[:, gsl, 1, d, :], xb[:, ts % 2, dc, :], start=(dc == 0), stop=(dc == n_dc - 1))
                    if d == HDC - 1:
                        gfree[k] = p.inc("tensor", fnu, "sT")
                    else:
                        p.emit("tensor", fnu)
            p.wait("scalar", "sT", gfree[2 * gic + 1])
            if gic - 2 >= 0:
                p.wait("scalar", "sV", u_used[gic - 2])
            g_evac[gic] = p.inc("scalar", lambda e, ic=ic, par=par, pg=pg: e.activation(
                gsc[:, par, :], pg, AF.Copy, scale=gs_us[:, ic:ic + 1]), "sS")
            p.wait("vector", "sS", g_evac[gic])
            u_used[gic] = p.inc("vector", lambda e, ic=ic, par=par, pu=pu: e.tensor_mul(
                gu[:, ic, :], pu, gsc[:, par, :]), "sV")
        p.wait("tensor", "sV", u_used[ts * n_ic + n_ic - 1])
        if "dbg_gu" + tag in t:
            p.wait("sync", "sV", u_used[ts * n_ic + n_ic - 1])
            p.inc("sync", lambda e, ts=ts: e.dma_start(t["dbg_gu" + tag][ts], gu[:, :, :]), "sA", 16)
            p.inc("sync", lambda e, ts=ts: e.dma_start(t["dbg_xb" + tag][ts], xb[:, ts % 2, :, :]), "sA", 16)
        base_k = ts * n_hc * NQ

        if "dbg_db" + tag in t and ts == 0:
            p.wait("sync", "sW", dready[base_k])
            p.inc("sync", lambda e: e.dma_start(t["dbg_db" + tag][:, :, :], dbuf[:, 0, :, :]), "sA", 16)
        for hc in range(n_hc):
            ghc = ts * n_hc + hc
            if ghc - 2 >= 0:
                p.wait("sync", "sV", ev_done[ghc - 2])
            a_r = p.inc("sync", lambda e, ts=ts, hc=hc: e.dma_start(
                xres[:, hc % 2, :], res[ts][:, hc, :]), "sA", 16)
            py = ps[:, (4 + hc % 2) * 512:(4 + hc % 2) * 512 + TT]
            if ghc - 2 >= 0:
                p.wait("tensor", "sV", ev_done[ghc - 2])
            for ic in range(n_ic):
                q = ic // QI
                kq = base_k + hc * NQ + q
                if ic % QI == 0:
                    p.wait("tensor", "sW", dready[kq])
                fn = lambda e, ic=ic, kq=kq, py=py: e.matmul(
                    py, dbuf[:, kq % 2, ic % QI, :], gu[:, ic, :],
                    start=(ic == 0), stop=(ic == n_ic - 1))
                if (ic + 1) % QI == 0 or ic == n_ic - 1:
                    dcons[kq] = p.inc("tensor", fn, "sT")
                    nk = kq + 2
                    if nk < base_k + n_hc * NQ and nk not in dready:
                        dready[nk] = emit_dload(nk)
                else:
                    p.emit("tensor", fn)
            t_y = dcons[base_k + hc * NQ + NQ - 1]
            p.wait("vector", "sT", t_y)
            p.wait("vector", "sA", a_r)
            if ghc - 2 >= 0:
                p.wait("vector", "sA", st_done[ghc - 2])
            if "dbg_y" + tag in t:
                p.wait("scalar", "sT", t_y)
                s_dy = p.inc("scalar", lambda e, py=py: e.activation(sb["ffn_dbgy"][:, :], py, AF.Copy), "sS")
                p.wait("sync", "sS", s_dy)
                a_dy = p.inc("sync", lambda e, ts=ts, hc=hc: e.dma_start(t["dbg_y" + tag][ts][:, hc, :], sb["ffn_dbgy"][:, :]), "sA", 16)
                p.wait("scalar", "sA", a_dy)
            ev_done[ghc] = p.inc("vector", lambda e, hc=hc, py=py: e.scalar_tensor_tensor(
                x1t[:, hc % 2, :], py, ds_half[:, hc:hc + 1], xres[:, hc % 2, :],
                ALU.mult, ALU.add), "sV")
            p.wait("sync", "sV", ev_done[ghc])
            st_done[ghc] = p.inc("sync", lambda e, ts=ts, hc=hc: e.dma_start(
                dst[ts][:, hc, :], x1t[:, hc % 2, :]), "sA", 16)


def emit_kv(p, cfg, t, sb, ps):
    n_dc, n_rc, TT, NT = cfg.n_dc, cfg.n_rc, cfg.TT, cfg.NT
    xb, wkv, wkr = sb["kv_xb"], sb["kv_wkv"], sb["kv_wkr"]
    cvt, tmt, krt, rtmp = sb["kv_cvt"], sb["kv_tmt"], sb["kv_krt"], sb["kv_rtmp"]
    ctab, stab = sb["cos_q"], sb["sin_q"]
    w_kv = p.inc("gpsimd", lambda e: e.dma_start(wkv[:, :, :], t["wkv_fm"][:, :, :]), "sW", 16)
    w_kr = p.inc("gpsimd", lambda e: e.dma_start(wkr[:, :, :], t["wkr_fm"][:, :, :]), "sW", 16)
    cv_evac, cv_st, tm_evac, tm_st = {}, {}, {}, {}
    t_tm_last = None
    v_kr = None
    for ts in range(NT):
        if ts > 0:
            p.wait("sync", "sT", t_tm_last)
        a_x = p.inc("sync", lambda e, ts=ts: e.dma_start(xb[:, :, :], t["xln2"][ts]), "sA", 16)
        p.wait("tensor", "sA", a_x)
        if ts == 0:
            p.wait("tensor", "sW", w_kv)
            p.wait("tensor", "sW", w_kr)
        for rc in range(n_rc):
            pb = ps[:, (rc % 2) * 512:(rc % 2) * 512 + TT]
            gi = ts * n_rc + rc
            if gi - 2 in cv_evac:
                p.wait("tensor", "sS", cv_evac[gi - 2])
            for dc in range(n_dc):
                fn = lambda e, rc=rc, dc=dc, pb=pb: e.matmul(
                    pb, wkv[:, dc, rc * 128:(rc + 1) * 128], xb[:, dc, :],
                    start=(dc == 0), stop=(dc == n_dc - 1))
                if dc == n_dc - 1:
                    t_cv = p.inc("tensor", fn, "sT")
                else:
                    p.emit("tensor", fn)
            p.wait("scalar", "sT", t_cv)
            if gi - 1 in cv_st:
                p.wait("scalar", "sA", cv_st[gi - 1])
            cv_evac[gi] = p.inc("scalar", lambda e, pb=pb: e.activation(cvt[:, :], pb, AF.Copy), "sS")
            p.wait("sync", "sS", cv_evac[gi])
            cv_st[gi] = p.inc("sync", lambda e, rc=rc, ts=ts: e.dma_start(
                t["kv_contrib"][rc * 128:(rc + 1) * 128, ts * TT:(ts + 1) * TT], cvt[:, :]), "sA", 16)
        pk1 = ps[0:32, 2 * 512:2 * 512 + TT]
        pk2 = ps[0:32, 5 * 512:5 * 512 + TT]
        if v_kr is not None:
            p.wait("tensor", "sV", v_kr)
        for dc in range(n_dc):
            p.emit("tensor", lambda e, dc=dc, pk1=pk1: e.matmul(
                pk1, wkr[:, dc, 0:32], xb[:, dc, :], start=(dc == 0), stop=(dc == n_dc - 1)))
            fn = lambda e, dc=dc, pk2=pk2: e.matmul(
                pk2, wkr[:, dc, 32:64], xb[:, dc, :], start=(dc == 0), stop=(dc == n_dc - 1))
            if dc == n_dc - 1:
                t_kr = p.inc("tensor", fn, "sT")
            else:
                p.emit("tensor", fn)
        c_sl = ctab[:, ts * TT:(ts + 1) * TT]
        s_sl = stab[:, ts * TT:(ts + 1) * TT]
        k0 = krt[0:32, 0, ts * TT:(ts + 1) * TT]
        k1 = krt[0:32, 1, ts * TT:(ts + 1) * TT]
        ra = rtmp[0:32, 0, :]
        rb = rtmp[0:32, 1, :]
        p.wait("vector", "sT", t_kr)
        p.emit("vector", lambda e, pk1=pk1, c_sl=c_sl, ra=ra: e.tensor_mul(ra, pk1, c_sl))
        p.emit("vector", lambda e, pk2=pk2, s_sl=s_sl, rb=rb: e.tensor_mul(rb, pk2, s_sl))
        p.emit("vector", lambda e, k0=k0, ra=ra, rb=rb: e.tensor_sub(k0, ra, rb))
        p.emit("vector", lambda e, pk1=pk1, s_sl=s_sl, ra=ra: e.tensor_mul(ra, pk1, s_sl))
        p.emit("vector", lambda e, pk2=pk2, c_sl=c_sl, rb=rb: e.tensor_mul(rb, pk2, c_sl))
        v_kr = p.inc("vector", lambda e, k1=k1, ra=ra, rb=rb: e.tensor_add(k1, ra, rb), "sV")
        p.wait("sync", "sV", v_kr)
        p.inc("sync", lambda e, ts=ts: e.dma_start(
            t["kv_contrib"][n_rc * 128:n_rc * 128 + 32, ts * TT:(ts + 1) * TT],
            krt[0:32, 0, ts * TT:(ts + 1) * TT]), "sA", 16)
        p.inc("sync", lambda e, ts=ts: e.dma_start(
            t["kv_contrib"][n_rc * 128 + 32:n_rc * 128 + 64, ts * TT:(ts + 1) * TT],
            krt[0:32, 1, ts * TT:(ts + 1) * TT]), "sA", 16)
        for tk in range(TT // 128):
            gi = ts * (TT // 128) + tk
            pb = ps[:, (3 + gi % 2) * 512:(3 + gi % 2) * 512 + cfg.KVL]
            if gi - 2 in tm_evac:
                p.wait("tensor", "sS", tm_evac[gi - 2])
            for dc in range(n_dc):
                fn = lambda e, tk=tk, dc=dc, pb=pb: e.matmul(
                    pb, xb[:, dc, tk * 128:(tk + 1) * 128], wkv[:, dc, :],
                    start=(dc == 0), stop=(dc == n_dc - 1))
                if dc == n_dc - 1:
                    t_tm = p.inc("tensor", fn, "sT")
                else:
                    p.emit("tensor", fn)
            t_tm_last = t_tm
            p.wait("scalar", "sT", t_tm)
            if gi - 1 in tm_st:
                p.wait("scalar", "sA", tm_st[gi - 1])
            tm_evac[gi] = p.inc("scalar", lambda e, pb=pb: e.activation(tmt[:, :], pb, AF.Copy), "sS")
            p.wait("sync", "sS", tm_evac[gi])
            tm_st[gi] = p.inc("sync", lambda e, ts=ts, tk=tk: e.dma_start(
                t["tm_contrib"][ts * TT + tk * 128:ts * TT + (tk + 1) * 128, :], tmt[:, :]), "sA", 16)
    p.wait("gpsimd", "sA", p.cnt["sA"])
    if cfg.n_mem == 1:
        p.inc("gpsimd", lambda e: e.dma_start(t["kv_gath"].ap()[:, :], t["kv_contrib"][:, :]), "sW", 16)
        p.inc("gpsimd", lambda e: e.dma_start(t["tm_gath"].ap()[:, :], t["tm_contrib"][:, :]), "sW", 16)
        p.wait("gpsimd", "sW", p.cnt["sW"])
    else:
        n_groups = getattr(cfg, "n_cores", 8) // cfg.n_mem
        groups = [[g * cfg.n_mem + m for m in range(cfg.n_mem)] for g in range(n_groups)]
        p.inc("gpsimd", lambda e, groups=groups: e.collective_compute(
            "AllGather", ALU.bypass, replica_groups=groups,
            ins=[t["kv_contrib"].opt()], outs=[t["kv_gath"].ap().opt()]), "sC1", 1)
        p.inc("gpsimd", lambda e, groups=groups: e.collective_compute(
            "AllGather", ALU.bypass, replica_groups=groups,
            ins=[t["tm_contrib"].opt()], outs=[t["tm_gath"].ap().opt()]), "sC2", 1)


def emit_load_gathered(p, cfg, t, sb, ps):
    n_rc, T, n_mem = cfg.n_rc, cfg.T, cfg.n_mem
    CK = n_rc * 128 + 64
    ckv, tm, kr = sb["at_ckv"], sb["at_tm"], sb["at_kr"]
    p.inc("gpsimd", lambda e: e.dma_start(sb["G"][:, :], t["G_h"][:, :]), "sW", 16)
    TT = cfg.TT
    if cfg.n_mem > 1:
        p.wait("sync", "sC1", 1)
    for m in range(n_mem):
        for ts in range(cfg.NT):
            gpos = ts * n_mem * TT + m * TT
            for rc in range(n_rc):
                p.inc("sync", lambda e, m=m, rc=rc, ts=ts, gpos=gpos: e.dma_start(
                    ckv[:, rc, gpos:gpos + TT],
                    t["kv_gath"].ap()[m * CK + rc * 128:m * CK + (rc + 1) * 128, ts * TT:(ts + 1) * TT]), "sA", 16)
            p.inc("sync", lambda e, m=m, ts=ts, gpos=gpos: e.dma_start(
                kr[0:32, 0, gpos:gpos + TT],
                t["kv_gath"].ap()[m * CK + n_rc * 128:m * CK + n_rc * 128 + 32, ts * TT:(ts + 1) * TT]), "sA", 16)
            p.inc("sync", lambda e, m=m, ts=ts, gpos=gpos: e.dma_start(
                kr[0:32, 1, gpos:gpos + TT],
                t["kv_gath"].ap()[m * CK + n_rc * 128 + 32:m * CK + n_rc * 128 + 64, ts * TT:(ts + 1) * TT]), "sA", 16)
    if cfg.n_mem > 1:
        p.wait("sync", "sC2", 1)
    TPC = TT // 128
    for ci in range(cfg.n_mem * cfg.T // 128):
        m, loc = divmod(ci, cfg.T // 128)
        ts, sub = divmod(loc, TPC)
        g = ts * n_mem * TPC + m * TPC + sub
        p.inc("sync", lambda e, ci=ci, g=g: e.dma_start(
            tm[:, g, :], t["tm_gath"].ap()[ci * 128:(ci + 1) * 128, :]), "sA", 16)
    a_keys = p.cnt["sA"]
    p.inc("vector", lambda e: e.memset(kr[32:33, 1, :], -1.0), "sV")
    p.inc("vector", lambda e: e.memset(sb["mk2"][:, :], 0.0), "sV")
    for eng in ("scalar", "vector", "tensor"):
        p.wait(eng, "sA", a_keys)
    p.wait("gpsimd", "sW", p.cnt["sW"])


def emit_maxk2(p, cfg, sb, ps):
    n_rc, T, n_mem = cfg.n_rc, cfg.T, cfg.n_mem
    ckv, kr = sb["at_ckv"], sb["at_kr"]
    sq, nrow, mk2 = sb["at_sq"], sb["at_nrow"], sb["mk2"]
    ones_bf = sb["ones_bf"]
    nch = (n_mem * T) // 512
    pn = ps[0:1, 2 * 512:2 * 512 + 512]
    v_ones = p.cnt["sV"]
    p.wait("tensor", "sV", v_ones)
    sq_done, mm_done = {}, {}
    v_mk = None
    k = 0
    for ch in range(nch):
        sl = slice(ch * 512, (ch + 1) * 512)
        srcs = [(ckv[:, rc, sl], 128) for rc in range(n_rc)] + [(kr[0:32, 0, sl], 32), (kr[0:32, 1, sl], 32)]
        nsrc = len(srcs)
        for i, (src, npart) in enumerate(srcs):
            slot = k % 2
            if k - 2 in mm_done:
                p.wait("scalar", "sT", mm_done[k - 2])
            sq_done[k] = p.inc("scalar", lambda e, src=src, npart=npart, slot=slot: e.activation(
                sq[0:npart, slot, :], src, AF.Square), "sS")
            p.wait("tensor", "sS", sq_done[k])
            if i == 0 and v_mk is not None:
                p.wait("tensor", "sV", v_mk)
            mm_done[k] = p.inc("tensor", lambda e, npart=npart, slot=slot, i=i, nsrc=nsrc: e.matmul(
                pn, ones_bf[0:npart, :], sq[0:npart, slot, :],
                start=(i == 0), stop=(i == nsrc - 1)), "sT")
            k += 1
        p.wait("vector", "sT", mm_done[k - 1])
        p.emit("vector", lambda e: e.reduce_max(nrow[0:1, 0:1], pn, axis=mybir.AxisListType.X))
        v_mk = p.inc("vector", lambda e: e.tensor_max(mk2[:, :], mk2[:, :], nrow[0:1, 0:1]), "sV")
    return v_mk


def emit_attention(p, cfg, t, sb, ps):
    n_dc, n_qc, n_rc, n_heads = cfg.n_dc, cfg.n_qc, cfg.n_rc, cfg.n_heads
    TT, NT, n_tc, SBLK = cfg.TT, cfg.NT, cfg.n_tc, cfg.SBLK
    xbq, attn = sb["at_xbq"], sb["at_attn"]
    qdbuf, obuf = sb["at_qdbuf"], sb["at_obuf"]
    nbuf, rbuf, ukbuf, vbuf = sb["at_nbuf"], sb["at_rbuf"], sb["at_ukbuf"], sb["at_vbuf"]
    cq, qn, qlat, qrope = sb["at_cq"], sb["at_qn"], sb["at_qlat"], sb["at_qrope"]
    probs, olb, invb, iden = sb["at_probs"], sb["at_olb"], sb["at_invb"], sb["at_iden"]
    iden32 = sb["at_iden32"]
    sq, rtmp = sb["at_sq"], sb["kv_rtmp"]
    xres, x2t = sb["at_xres"], sb["at_x2t"]
    ckv, tm, kr = sb["at_ckv"], sb["at_tm"], sb["at_kr"]
    G, mk2 = sb["G"], sb["mk2"]
    ctab, stab = sb["cos_q"], sb["sin_q"]
    ones_bf, ones1_f32 = sb["ones_bf"], sb["ones1_f32"]
    hw, wo, cqmm, cqev, at_ev = {}, {}, {}, {}, {}
    bg_cast_i = [0]
    bg_cast_done = [None]
    wo_ev, wo_st = {}, {}
    pv_done = {}
    v_iden_prev = None
    pending_tail = None
    prev_exp_last = None
    qn_ev = None
    qr_rot = None
    sc_last = None
    first_qnorm = True
    for sb_i in range(NT):
        if (sb_i - 1, n_dc - 1) in wo:
            p.wait("sync", "sT", wo[(sb_i - 1, n_dc - 1)])
        a_x = p.inc("sync", lambda e, sb_i=sb_i: e.dma_start(xbq[:, :, :], t["xln2"][sb_i]), "sA", 16)
        for qc in range(n_qc):
            par = qc % 2
            gi = (sb_i, qc)
            if (sb_i, qc - 2) in cqmm:
                p.wait("gpsimd", "sT", cqmm[(sb_i, qc - 2)])
            elif sb_i > 0 and (sb_i - 1, n_dc - 1) in wo:
                p.wait("gpsimd", "sT", wo[(sb_i - 1, n_dc - 1)])
            w_q = p.inc("gpsimd", lambda e, qc=qc, par=par: e.dma_start(
                qdbuf[:, par, :, :], t["wqd_t"][qc]), "sW", 16)
            p.wait("tensor", "sW", w_q)
            if qc == 0:
                p.wait("tensor", "sA", a_x)
            pb = ps[:, par * 512:par * 512 + SBLK]
            if (sb_i, qc - 2) in cqev:
                p.wait("tensor", "sS", cqev[(sb_i, qc - 2)])
            elif qc < 2 and sb_i > 0:
                p.wait("tensor", "sS", at_ev[sb_i * n_heads - 1])
            for dc in range(n_dc):
                fn = lambda e, qc=qc, dc=dc, par=par, pb=pb: e.matmul(
                    pb, qdbuf[:, par, dc, :], xbq[:, dc, :], start=(dc == 0), stop=(dc == n_dc - 1))
                if dc == n_dc - 1:
                    cqmm[gi] = p.inc("tensor", fn, "sT")
                else:
                    p.emit("tensor", fn)
            p.wait("scalar", "sT", cqmm[gi])
            cqev[gi] = p.inc("scalar", lambda e, qc=qc, pb=pb: e.activation(cq[:, qc, :], pb, AF.Copy), "sS")
        if sb_i == 0:
            emit_load_gathered(p, cfg, t, sb, ps)
            v_mk = emit_maxk2(p, cfg, sb, ps)
            p.wait("tensor", "sV", v_mk)
            t_mkb = p.inc("tensor", lambda e: e.matmul(
                ps[:, 7 * 512:7 * 512 + 1], ones1_f32[:, :], mk2[0:1, 0:1], start=True, stop=True), "sT")
            p.wait("vector", "sT", t_mkb)
            v_mk32 = p.inc("vector", lambda e: e.tensor_copy(
                sb["mk2_32"][32:33, 0:1], ps[32:33, 7 * 512:7 * 512 + 1]), "sV")
        for h in range(n_heads):
            hg = sb_i * n_heads + h
            hp = h % 2
            if hg - 2 in hw:
                p.wait("gpsimd", "sT", hw[hg - 2])
            if h < 2:
                p.wait("gpsimd", "sT", cqmm[(sb_i, n_qc - 1)])
            for _ in range(2):
                ci = bg_cast_i[0]
                if ci < cfg.n_ic + cfg.n_hc:
                    bg_cast_i[0] += 1
                    if ci < cfg.n_ic:
                        bg_cast_done[0] = p.inc("gpsimd", lambda e, ci=ci: e.dma_start(
                            t["guw_bf"][ci], t["guw_t"][ci]), "sW", 16)
                    else:
                        hc2 = ci - cfg.n_ic
                        bg_cast_done[0] = p.inc("gpsimd", lambda e, hc2=hc2: e.dma_start(
                            t["down_bf"][hc2], t["down_t"][hc2]), "sW", 16)
            w_n = p.inc("gpsimd", lambda e, h=h, hp=hp: e.dma_start(nbuf[:, hp, :, :], t["wqun_t"][h]), "sW", 16)
            w_r = p.inc("gpsimd", lambda e, h=h, hp=hp: e.dma_start(rbuf[:, hp, :, :], t["wqur_t"][h]), "sW", 16)
            w_k = p.inc("gpsimd", lambda e, h=h, hp=hp: e.dma_start(ukbuf[:, hp, :], t["wuk_fm"][h]), "sW", 16)
            w_v = p.inc("gpsimd", lambda e, h=h, hp=hp: e.dma_start(vbuf[:, hp, :, :], t["wuv_tt"][h]), "sW", 16)
            p.wait("tensor", "sW", w_n)
            p.wait("tensor", "sS", cqev[(sb_i, n_qc - 1)])
            if qn_ev is not None:
                p.wait("tensor", "sS", qn_ev)
            if prev_exp_last is not None:
                p.wait("tensor", "sS", prev_exp_last)
            pqn = ps[:, 0:SBLK]
            for qc in range(n_qc):
                fn = lambda e, qc=qc, hp=hp, pqn=pqn: e.matmul(
                    pqn, nbuf[:, hp, qc, :], cq[:, qc, :], start=(qc == 0), stop=(qc == n_qc - 1))
                if qc == n_qc - 1:
                    t_qn = p.inc("tensor", fn, "sT")
                else:
                    p.emit("tensor", fn)
            p.wait("scalar", "sT", t_qn)
            qn_ev = p.inc("scalar", lambda e, pqn=pqn: e.activation(qn[:, :], pqn, AF.Copy), "sS")
            p.wait("tensor", "sW", w_r)
            if qr_rot is not None:
                p.wait("tensor", "sV", qr_rot)
            p.wait("tensor", "sS", qn_ev)
            pq1 = ps[0:32, 512:512 + SBLK]
            pq2 = ps[0:32, 0:SBLK]
            for qc in range(n_qc):
                p.emit("tensor", lambda e, qc=qc, hp=hp, pq1=pq1: e.matmul(
                    pq1, rbuf[:, hp, qc, 0:32], cq[:, qc, :], start=(qc == 0), stop=(qc == n_qc - 1)))
                fn = lambda e, qc=qc, hp=hp, pq2=pq2: e.matmul(
                    pq2, rbuf[:, hp, qc, 32:64], cq[:, qc, :], start=(qc == 0), stop=(qc == n_qc - 1))
                if qc == n_qc - 1:
                    t_qr = p.inc("tensor", fn, "sT")
                else:
                    p.emit("tensor", fn)
            c_sl = ctab[:, sb_i * SBLK:(sb_i + 1) * SBLK]
            s_sl = stab[:, sb_i * SBLK:(sb_i + 1) * SBLK]
            ra = rtmp[0:32, 0, 0:SBLK]
            rb = rtmp[0:32, 1, 0:SBLK]
            p.wait("vector", "sT", t_qr)
            if sc_last is not None:
                p.wait("vector", "sT", sc_last)
            p.emit("vector", lambda e, pq1=pq1, c_sl=c_sl, ra=ra: e.tensor_mul(ra, pq1, c_sl))
            p.emit("vector", lambda e, pq2=pq2, s_sl=s_sl, rb=rb: e.tensor_mul(rb, pq2, s_sl))
            p.emit("vector", lambda e, ra=ra, rb=rb: e.tensor_sub(qrope[0:32, 0, :], ra, rb))
            p.emit("vector", lambda e, pq1=pq1, s_sl=s_sl, ra=ra: e.tensor_mul(ra, pq1, s_sl))
            p.emit("vector", lambda e, pq2=pq2, c_sl=c_sl, rb=rb: e.tensor_mul(rb, pq2, c_sl))
            qr_rot = p.inc("vector", lambda e, ra=ra, rb=rb: e.tensor_add(qrope[0:32, 1, :], ra, rb), "sV")
            p.wait("tensor", "sW", w_k)
            p.wait("tensor", "sS", qn_ev)
            ql_ev = [None] * n_rc
            for rc in range(n_rc):
                pb = ps[:, (rc % 2) * 512:(rc % 2) * 512 + SBLK]
                if rc == 0:
                    p.wait("tensor", "sV", qr_rot)
                if rc >= 2:
                    p.wait("tensor", "sS", ql_ev[rc - 2])
                t_ql = p.inc("tensor", lambda e, rc=rc, hp=hp, pb=pb: e.matmul(
                    pb, ukbuf[:, hp, rc * 128:(rc + 1) * 128], qn[:, :], start=True, stop=True), "sT")
                p.wait("scalar", "sT", t_ql)
                ql_ev[rc] = p.inc("scalar", lambda e, rc=rc, pb=pb: e.activation(qlat[:, rc, :], pb, AF.Copy), "sS")
            if pending_tail is not None:
                pending_tail()
                pending_tail = None
            if first_qnorm:
                p.wait("tensor", "sV", v_mk)
                first_qnorm = False
            elif v_iden_prev is not None:
                p.wait("tensor", "sV", v_iden_prev)
            qnorm_mm = {}
            for rc in range(n_rc):
                slot = rc % 2
                if rc - 2 >= 0:
                    p.wait("scalar", "sT", qnorm_mm[rc - 2])
                s_sq = p.inc("scalar", lambda e, rc=rc, slot=slot: e.activation(
                    sq[:, slot, 0:SBLK], qlat[:, rc, :], AF.Square), "sS")
                p.wait("tensor", "sS", s_sq)
                qnorm_mm[rc] = p.inc("tensor", lambda e, rc=rc, slot=slot: e.matmul(
                    ps[32:33, 2 * 512:2 * 512 + SBLK], ones_bf[:, :], sq[:, slot, 0:SBLK],
                    start=(rc == 0), stop=False, tile_position=(0, 32)), "sT")
            p.wait("scalar", "sV", qr_rot)
            s_sqr = p.inc("scalar", lambda e: e.activation(
                sq[0:32, 2, 0:SBLK], qrope[0:32, 0, :], AF.Square), "sS")
            s_sqr2 = p.inc("scalar", lambda e: e.activation(
                sq[32:64, 2, 0:SBLK], qrope[0:32, 1, :], AF.Square), "sS") if False else None
            p.wait("tensor", "sS", s_sqr)
            p.inc("tensor", lambda e: e.matmul(
                ps[32:33, 2 * 512:2 * 512 + SBLK], ones_bf[0:32, :], sq[0:32, 2, 0:SBLK],
                start=False, stop=False, tile_position=(0, 32)), "sT")
            p.wait("scalar", "sT", p.cnt["sT"])
            s_sqr3 = p.inc("scalar", lambda e: e.activation(
                sq[0:32, 2, 0:SBLK], qrope[0:32, 1, :], AF.Square), "sS")
            p.wait("tensor", "sS", s_sqr3)
            t_n2 = p.inc("tensor", lambda e: e.matmul(
                ps[32:33, 2 * 512:2 * 512 + SBLK], ones_bf[0:32, :], sq[0:32, 2, 0:SBLK],
                start=False, stop=True, tile_position=(0, 32)), "sT")
            p.wait("vector", "sT", t_n2)
            v_n2 = p.inc("vector", lambda e: e.tensor_scalar(
                iden32[32:33, SBLK:2 * SBLK], ps[32:33, 2 * 512:2 * 512 + SBLK],
                sb["mk2_32"][32:33, 0:1], None, ALU.mult), "sV")
            p.wait("scalar", "sV", v_n2)
            p.emit("scalar", lambda e: e.activation(iden32[32:33, 0:SBLK], iden32[32:33, SBLK:2 * SBLK], AF.Ln))
            if sc_last is not None:
                p.wait("scalar", "sT", sc_last)
            s_brow = p.inc("scalar", lambda e: e.activation(
                qrope[32:33, 1, :], iden32[32:33, 0:SBLK], AF.Exp, scale=0.5, bias=sb["cst"][0:1, 1:2]), "sS")
            exp_done = [None] * n_tc
            brow_tok = s_brow

            n_run = cfg.LSB[sb_i]

            def emit_pv(tc):
                p.wait("tensor", "sS", exp_done[tc])
                p.emit("tensor", lambda e, tc=tc, n_run=n_run: e.matmul(
                    ps[0:1, 2 * 512:2 * 512 + SBLK], ones_bf[:, :], probs[:, tc % 4, :],
                    start=(tc == 0), stop=(tc == n_run - 1)))
                for rc in range(n_rc):
                    fn = lambda e, tc=tc, rc=rc, n_run=n_run: e.matmul(
                        ps[:, (3 + rc) * 512:(3 + rc) * 512 + SBLK], tm[:, tc, rc * 128:(rc + 1) * 128],
                        probs[:, tc % 4, :], start=(tc == 0), stop=(tc == n_run - 1))
                    if rc == n_rc - 1:
                        pv_done[(hg, tc)] = p.inc("tensor", fn, "sT")
                    else:
                        p.emit("tensor", fn)

            for tc in range(n_run):
                pb = ps[:, (tc % 2) * 512:(tc % 2) * 512 + SBLK]
                if tc >= 2:
                    p.wait("tensor", "sS", exp_done[tc - 2])
                for rc in range(n_rc):
                    p.emit("tensor", lambda e, tc=tc, rc=rc, pb=pb: e.matmul(
                        pb, ckv[:, rc, tc * 128:(tc + 1) * 128], qlat[:, rc, :],
                        start=(rc == 0), stop=False))
                if tc == 0:
                    p.wait("tensor", "sS", brow_tok)
                p.emit("tensor", lambda e, tc=tc, pb=pb: e.matmul(
                    pb, kr[0:32, 0, tc * 128:(tc + 1) * 128], qrope[0:32, 0, :], start=False, stop=False))
                sc_stop = p.inc("tensor", lambda e, tc=tc, pb=pb: e.matmul(
                    pb, kr[0:33, 1, tc * 128:(tc + 1) * 128], qrope[0:33, 1, :], start=False, stop=True), "sT")
                sc_last = sc_stop
                a0 = cfg.GA + tc * 128 - sb_i * cfg.n_mem * TT
                Gv = G[:, a0:a0 - SBLK:-1] if a0 - SBLK >= 0 else G[:, a0::-1]
                p.wait("vector", "sT", sc_stop)
                mask_d = p.inc("vector", lambda e, pb=pb, Gv=Gv: e.tensor_add(pb, pb, Gv), "sV")
                p.wait("scalar", "sV", mask_d)
                if tc >= 4:
                    p.wait("scalar", "sT", pv_done[(hg, tc - 4)])
                exp_done[tc] = p.inc("scalar", lambda e, tc=tc, pb=pb: e.activation(
                    probs[:, tc % 4, :], pb, AF.Exp, scale=float(cfg.scale)), "sS")
                if tc >= 1:
                    emit_pv(tc - 1)
            emit_pv(n_run - 1)
            prev_exp_last = exp_done[n_run - 1]
            def make_tail(hg=hg, h=h, hp=hp, w_v=w_v):
                def tail():
                    nonlocal v_iden_prev
                    p.wait("vector", "sT", pv_done[(hg, cfg.LSB[hg // n_heads] - 1)])
                    v_iden = p.inc("vector", lambda e: e.reciprocal(iden[0:1, 0:SBLK], ps[0:1, 2 * 512:2 * 512 + SBLK]), "sV")
                    v_iden_prev = v_iden
                    p.wait("tensor", "sV", v_iden)
                    if hg >= 1:
                        p.wait("tensor", "sS", at_ev[hg - 1])
                    t_bc = p.inc("tensor", lambda e: e.matmul(
                        ps[:, 7 * 512:7 * 512 + SBLK], ones1_f32[:, :], iden[0:1, 0:SBLK], start=True, stop=True), "sT")
                    p.wait("vector", "sT", t_bc)
                    p.inc("vector", lambda e: e.tensor_copy(invb[:, :], ps[:, 7 * 512:7 * 512 + SBLK]), "sV")
                    ol_ev = [None] * n_rc
                    for rc in range(n_rc):
                        ol_ev[rc] = p.inc("vector", lambda e, rc=rc: e.tensor_mul(
                            olb[:, rc, :], ps[:, (3 + rc) * 512:(3 + rc) * 512 + SBLK], invb[:, :]), "sV")
                    p.wait("tensor", "sV", ol_ev[n_rc - 1])
                    p.wait("tensor", "sW", w_v)
                    for rc in range(n_rc):
                        fn = lambda e, rc=rc, hp=hp: e.matmul(
                            ps[:, 7 * 512:7 * 512 + SBLK], vbuf[:, hp, rc, :], olb[:, rc, :],
                            start=(rc == 0), stop=(rc == n_rc - 1))
                        if rc == n_rc - 1:
                            hw[hg] = p.inc("tensor", fn, "sT")
                        else:
                            p.emit("tensor", fn)
                    p.wait("scalar", "sT", hw[hg])
                    at_ev[hg] = p.inc("scalar", lambda e, h=h: e.activation(
                        attn[:, h, :], ps[:, 7 * 512:7 * 512 + SBLK], AF.Copy), "sS")
                return tail
            pending_tail = make_tail()
        pending_tail()
        pending_tail = None
        if sb_i == NT - 1:
            while bg_cast_i[0] < cfg.n_ic + cfg.n_hc:
                ci = bg_cast_i[0]
                bg_cast_i[0] += 1
                if ci < cfg.n_ic:
                    bg_cast_done[0] = p.inc("gpsimd", lambda e, ci=ci: e.dma_start(
                        t["guw_bf"][ci], t["guw_t"][ci]), "sW", 16)
                else:
                    hc2 = ci - cfg.n_ic
                    bg_cast_done[0] = p.inc("gpsimd", lambda e, hc2=hc2: e.dma_start(
                        t["down_bf"][hc2], t["down_t"][hc2]), "sW", 16)
        p.wait("tensor", "sS", at_ev[sb_i * n_heads + n_heads - 1])
        for oc in range(n_dc):
            par = oc % 2
            goc = sb_i * n_dc + oc
            if (sb_i, oc - 2) in wo:
                p.wait("gpsimd", "sT", wo[(sb_i, oc - 2)])
            elif oc < 2:
                p.wait("gpsimd", "sT", hw[sb_i * n_heads + n_heads - 1])
            w_o = p.inc("gpsimd", lambda e, oc=oc, par=par: e.dma_start(obuf[:, par, :, :], t["wout_t"][oc]), "sW", 16)
            if goc - 2 in wo_ev:
                p.wait("sync", "sV", wo_ev[goc - 2])
            a_r = p.inc("sync", lambda e, sb_i=sb_i, oc=oc: e.dma_start(
                xres[:, oc % 2, :], t["x1_fm"][sb_i][:, oc, :]), "sA", 16)
            p.wait("tensor", "sW", w_o)
            pb = ps[:, par * 512:par * 512 + SBLK]
            if goc - 2 in wo_ev:
                p.wait("tensor", "sV", wo_ev[goc - 2])
            for hc in range(n_heads):
                fn = lambda e, oc=oc, hc=hc, par=par, pb=pb: e.matmul(
                    pb, obuf[:, par, hc, :], attn[:, hc, :], start=(hc == 0), stop=(hc == n_heads - 1))
                if hc == n_heads - 1:
                    wo[(sb_i, oc)] = p.inc("tensor", fn, "sT")
                else:
                    p.emit("tensor", fn)
            p.wait("vector", "sT", wo[(sb_i, oc)])
            p.wait("vector", "sA", a_r)
            if goc - 2 in wo_st:
                p.wait("vector", "sA", wo_st[goc - 2])
            wo_ev[goc] = p.inc("vector", lambda e, oc=oc, pb=pb: e.tensor_add(
                x2t[:, oc % 2, :], pb, xres[:, oc % 2, :]), "sV")
            p.wait("sync", "sV", wo_ev[goc])
            wo_st[goc] = p.inc("sync", lambda e, sb_i=sb_i, oc=oc: e.dma_start(
                t["x2_fm"][sb_i][:, oc, :], x2t[:, oc % 2, :]), "sA", 16)


def build_program(cfg: Cfg):
    nc = bass.Bass("TRN2", target_bir_lowering=False)
    p = Prog(nc)
    for s in ("sW", "sA", "sT", "sV", "sS", "sC1", "sC2"):
        p.sem(s)
    for e in ENGINES:
        p.sem("bar_" + e)

    n_dc, n_ic, n_qc, n_rc, n_heads = cfg.n_dc, cfg.n_ic, cfg.n_qc, cfg.n_rc, cfg.n_heads
    TT, NT, T = cfg.TT, cfg.NT, cfg.T
    CK = n_rc * 128 + 64

    t = {}
    def ein(name, shape, dtype):
        t[name] = nc.dram_tensor(name, shape, dtype, kind="ExternalInput").ap()
    ein("x_fm", [NT, 128, n_dc, TT], F32)
    ein("guw_t", [n_ic, 128, 2, n_dc, 128], I32)
    ein("down_t", [cfg.n_hc, 128, n_ic, 128], I32)
    ein("gs_us_h", [128, n_ic], F32)
    ein("ds_half_h", [128, cfg.n_hc], F32)
    ein("wqd_t", [n_qc, 128, n_dc, 128], F32)
    ein("wqun_t", [n_heads, 128, n_qc, 128], F32)
    ein("wqur_t", [n_heads, 128, n_qc, 64], F32)
    ein("wkv_fm", [128, n_dc, n_rc * 128], F32)
    ein("wkr_fm", [128, n_dc, 64], F32)
    ein("wuk_fm", [n_heads, 128, n_rc * 128], F32)
    ein("wuv_tt", [n_heads, 128, n_rc, 128], F32)
    ein("wout_t", [n_dc, 128, n_heads, 128], F32)
    ein("cos_q_h", [32, T], F32)
    ein("sin_q_h", [32, T], F32)
    ein("G_h", [128, cfg.GLEN], F32)
    t["y_fm"] = nc.dram_tensor("y_fm", [NT, 128, n_dc, TT], F32, kind="ExternalOutput").ap()
    import os as _os
    _k2 = "ExternalOutput" if _os.environ.get("MK_DEBUG2") else "Internal"
    t["xln1"] = nc.dram_tensor("xln1", [NT, 128, n_dc, TT], BF16, kind=_k2).ap()
    t["xln2"] = nc.dram_tensor("xln2", [NT, 128, n_dc, TT], BF16, kind=_k2).ap()
    t["xln3"] = nc.dram_tensor("xln3", [NT, 128, n_dc, TT], BF16, kind=_k2).ap()
    t["x1_fm"] = nc.dram_tensor("x1_fm", [NT, 128, n_dc, TT], F32, kind=_k2).ap()
    t["x2_fm"] = nc.dram_tensor("x2_fm", [NT, 128, n_dc, TT], F32, kind=_k2).ap()
    import os
    if os.environ.get("MK_DEBUG"):
        for tg in ("_f1", "_f2"):
            t["dbg_gu" + tg] = nc.dram_tensor("dbg_gu" + tg, [NT, 128, n_ic, TT], BF16).ap()
            t["dbg_xb" + tg] = nc.dram_tensor("dbg_xb" + tg, [NT, 128, n_dc, TT], BF16).ap()
            t["dbg_xb2" + tg] = nc.dram_tensor("dbg_xb2" + tg, [NT, 128, n_dc, TT], BF16).ap()
            t["dbg_db" + tg] = nc.dram_tensor("dbg_db" + tg, [128, (n_ic + 3) // 4, 128], BF16).ap()
            t["dbg_y" + tg] = nc.dram_tensor("dbg_y" + tg, [NT, 128, n_dc, TT], F32).ap()
    t["guw_bf"] = nc.dram_tensor("guw_bf", [n_ic, 128, 2, n_dc, 128], BF16).ap()
    t["down_bf"] = nc.dram_tensor("down_bf", [cfg.n_hc, 128, n_ic, 128], BF16).ap()
    t["kv_contrib"] = nc.dram_tensor("kv_contrib", [CK, T], BF16).ap()
    t["tm_contrib"] = nc.dram_tensor("tm_contrib", [T, n_rc * 128], BF16).ap()
    t["kv_gath"] = nc.dram_tensor("kv_gath", [cfg.n_mem * CK, T], BF16)
    t["tm_gath"] = nc.dram_tensor("tm_gath", [cfg.n_mem * T, n_rc * 128], BF16)

    psu = nc.alloc_psum_tensor("PSU", [128, 4096], F32).ap()

    sb = {}
    guards = {"persist": [], "ln": [], "ffn": [], "kv": [], "at": []}

    nname = [0]
    def ph_alloc(group, name, shape, dtype):
        nname[0] += 1
        g = nc.sbuf_tensor(f"{name}__{nname[0]}", shape, dtype)
        h = g.__enter__()
        guards[group].append(g)
        sb[name] = h.ap()
        return h

    def ph_free(group):
        for g in reversed(guards[group]):
            g.__exit__(None, None, None)
        guards[group] = []

    ph_alloc("persist", "ones_bf", [128, 1], BF16)
    ph_alloc("persist", "ones1_f32", [1, 128], F32)
    ph_alloc("persist", "gs_us", [128, n_ic], F32)
    ph_alloc("persist", "ds_half", [128, cfg.n_hc], F32)
    ph_alloc("persist", "cos_q", [32, T], F32)
    ph_alloc("persist", "sin_q", [32, T], F32)
    ph_alloc("persist", "mk2", [1, 1], F32)
    ph_alloc("persist", "cst", [1, 2], F32)

    p.emit("vector", lambda e: e.memset(sb["ones_bf"][:, :], 1.0))
    p.emit("vector", lambda e: e.memset(sb["cst"][0:1, 0:1], float(cfg.eps)))
    p.emit("vector", lambda e: e.memset(sb["cst"][0:1, 1:2], 0.02))
    p.inc("vector", lambda e: e.memset(sb["ones1_f32"][:, :], 1.0), "sV")
    p.inc("sync", lambda e: e.dma_start(sb["gs_us"][:, :], t["gs_us_h"][:, :]), "sA", 16)
    p.inc("sync", lambda e: e.dma_start(sb["ds_half"][:, :], t["ds_half_h"][:, :]), "sA", 16)
    p.inc("sync", lambda e: e.dma_start(sb["cos_q"][:, :], t["cos_q_h"][:, :]), "sA", 16)
    p.inc("sync", lambda e: e.dma_start(sb["sin_q"][:, :], t["sin_q_h"][:, :]), "sA", 16)
    p.barrier()

    def alloc_ln():
        ph_alloc("ln", "ln_xf", [128, n_dc, TT], F32)
        ph_alloc("ln", "ln_xb", [128, n_dc, TT], BF16)
        ph_alloc("ln", "ln_xsq", [128, n_dc, TT], BF16)
        ph_alloc("ln", "ln_ab", [128, TT], BF16)
        ph_alloc("ln", "ln_bb", [128, TT], BF16)
        ph_alloc("ln", "ln_rows", [1, 6 * TT], F32)

    HDC = n_dc // 2
    QI = (n_ic + 3) // 4
    def alloc_ffn():
        ph_alloc("ffn", "ffn_xb", [128, 2, n_dc, TT], BF16)
        ph_alloc("ffn", "ffn_guw", [128, 3, 2, HDC, 128], BF16)
        ph_alloc("ffn", "ffn_dbuf", [128, 2, QI, 128], BF16)
        ph_alloc("ffn", "ffn_gu", [128, n_ic, TT], BF16)
        ph_alloc("ffn", "ffn_gsc", [128, 2, TT], BF16)
        ph_alloc("ffn", "ffn_xres", [128, 2, TT], F32)
        ph_alloc("ffn", "ffn_x1t", [128, 2, TT], F32)
        ph_alloc("ffn", "ffn_dbgy", [128, TT], F32)

    alloc_ln()
    emit_layernorm(p, cfg, t["x_fm"], t["xln1"], sb, psu)
    p.barrier()
    ph_free("ln")
    alloc_ffn()
    emit_ffn(p, cfg, t, sb, psu, "xln1", "x_fm", "x1_fm", tag="_f1")
    p.barrier()
    ph_free("ffn")
    alloc_ln()
    emit_layernorm(p, cfg, t["x1_fm"], t["xln2"], sb, psu)
    p.barrier()
    ph_free("ln")

    ph_alloc("kv", "kv_xb", [128, n_dc, TT], BF16)
    ph_alloc("kv", "kv_wkv", [128, n_dc, n_rc * 128], BF16)
    ph_alloc("kv", "kv_wkr", [128, n_dc, 64], BF16)
    ph_alloc("kv", "kv_cvt", [128, TT], BF16)
    ph_alloc("kv", "kv_tmt", [128, n_rc * 128], BF16)
    ph_alloc("kv", "kv_krt", [32, 2, T], BF16)
    ph_alloc("kv", "kv_rtmp", [32, 2, TT], F32)
    emit_kv(p, cfg, t, sb, psu)
    p.barrier()
    ph_free("kv")

    R = ph_alloc("at", "at_R", [128, n_dc * TT], BF16)
    sb["at_xbq"] = R.ap().rearrange("p (a b) -> p a b", a=n_dc)
    sb["at_attn"] = R.ap().rearrange("p (a b) -> p a b", a=n_heads)
    wcols = max(2 * max(n_heads, n_dc) * 128,
                2 * n_qc * 128 + 2 * n_qc * 64 + 4 * n_rc * 128)
    W = ph_alloc("at", "at_W", [128, wcols], BF16)
    sb["at_qdbuf"] = W.ap()[:, 0:2 * n_dc * 128].rearrange("p (a b c) -> p a b c", a=2, b=n_dc)
    sb["at_obuf"] = W.ap()[:, 0:2 * n_heads * 128].rearrange("p (a b c) -> p a b c", a=2, b=n_heads)
    hwap = W.ap()
    o1 = 2 * n_qc * 128
    o2 = o1 + 2 * n_qc * 64
    o3 = o2 + 2 * n_rc * 128
    sb["at_nbuf"] = hwap[:, 0:o1].rearrange("p (a b c) -> p a b c", a=2, b=n_qc)
    sb["at_rbuf"] = hwap[:, o1:o2].rearrange("p (a b c) -> p a b c", a=2, b=n_qc)
    sb["at_ukbuf"] = hwap[:, o2:o3].rearrange("p (a b) -> p a b", a=2)
    sb["at_vbuf"] = hwap[:, o3:o3 + 2 * n_rc * 128].rearrange("p (a b c) -> p a b c", a=2, b=n_rc)
    ph_alloc("at", "at_ckv", [128, n_rc, cfg.n_mem * T], BF16)
    ph_alloc("at", "at_tm", [128, cfg.n_tc, n_rc * 128], BF16)
    ph_alloc("at", "at_kr", [33, 2, cfg.n_mem * T], BF16)
    ph_alloc("at", "G", [128, cfg.GLEN], BF16)
    ph_alloc("at", "at_sq", [128, 3, 512], BF16)
    ph_alloc("at", "at_nrow", [1, 2], F32)
    ph_alloc("at", "at_cq", [128, n_qc, cfg.SBLK], BF16)
    ph_alloc("at", "at_qn", [128, cfg.SBLK], BF16)
    ph_alloc("at", "at_qlat", [128, n_rc, cfg.SBLK], BF16)
    ph_alloc("at", "at_qrope", [33, 2, cfg.SBLK], BF16)
    ph_alloc("at", "at_probs", [128, 4, cfg.SBLK], BF16)
    ph_alloc("at", "at_olb", [128, n_rc, cfg.SBLK], BF16)
    ph_alloc("at", "at_invb", [128, cfg.SBLK], F32)
    ph_alloc("at", "at_iden", [1, 2 * cfg.SBLK], F32)
    ph_alloc("at", "at_iden32", [33, 2 * cfg.SBLK], F32)
    ph_alloc("at", "mk2_32", [33, 1], F32)
    ph_alloc("at", "at_xres", [128, 2, TT], F32)
    ph_alloc("at", "at_x2t", [128, 2, TT], F32)
    ph_alloc("at", "kv_rtmp", [32, 2, TT], F32)
    emit_attention(p, cfg, t, sb, psu)
    p.barrier()
    ph_free("at")

    alloc_ln()
    emit_layernorm(p, cfg, t["x2_fm"], t["xln3"], sb, psu)
    p.barrier()
    ph_free("ln")
    alloc_ffn()
    emit_ffn(p, cfg, t, sb, psu, "xln3", "x2_fm", "y_fm", tag="_f2", use_bf=True)
    p.wait("sync", "sA", p.cnt["sA"])
    p.barrier()

    p.replay()
    p.close()
    return nc


def yarn_tables(cfg, positions):
    DR = cfg.dr
    freqs = 1.0 / 10000.0 ** (np.arange(0, DR, 2, dtype=np.float64) / DR)
    wl = 2.0 * np.pi / freqs
    ramp = np.clip((wl / 4096.0 - 1.0) / (32.0 - 1.0), 0.0, 1.0)
    scale = 1.0 - ramp + ramp * 40.0
    inv = freqs / scale
    f = positions[None, :].astype(np.float64) * inv[:, None]
    return np.cos(f).astype(np.float32), np.sin(f).astype(np.float32)


def core_positions(cfg, core):
    mi = core % cfg.n_mem
    return [np.arange(ts * cfg.n_mem * cfg.TT + mi * cfg.TT,
                      ts * cfg.n_mem * cfg.TT + (mi + 1) * cfg.TT) for ts in range(cfg.NT)]


def host_prep(cfg: Cfg, inputs, core):
    n_dc, n_ic, n_qc, n_rc, n_heads = cfg.n_dc, cfg.n_ic, cfg.n_qc, cfg.n_rc, cfg.n_heads
    TT, NT, T = cfg.TT, cfg.NT, cfg.T
    b = core // cfg.n_mem
    posl = core_positions(cfg, core)
    x = inputs["L0_attn_hidden_states"]
    tok = np.concatenate([np.asarray(x[b, p[0]:p[0] + TT, :]) for p in posl], axis=0)
    m = {}
    m["x_fm"] = np.ascontiguousarray(tok.reshape(NT, TT, n_dc, 128).transpose(0, 3, 2, 1))
    gate = inputs["L0_ffn_gate_wq"]; up = inputs["L0_ffn_up_wq"]; down = inputs["L0_ffn_down_wq"]
    gt = gate.reshape(n_ic, 128, n_dc, 128)
    ut = up.reshape(n_ic, 128, n_dc, 128)
    guw = np.stack([gt, ut], axis=1)
    m["guw_t"] = np.ascontiguousarray(guw.transpose(0, 4, 1, 3, 2))
    dt_ = down.reshape(cfg.n_hc, 128, n_ic, 128)
    m["down_t"] = np.ascontiguousarray(dt_.transpose(0, 3, 2, 1))
    gs = inputs["L0_ffn_gate_scale"]; us = inputs["L0_ffn_up_scale"]; ds = inputs["L0_ffn_down_scale"]
    m["gs_us_h"] = np.ascontiguousarray((gs * us).reshape(n_ic, 128).T)
    m["ds_half_h"] = np.ascontiguousarray((0.5 * ds).reshape(cfg.n_hc, 128).T)
    wqd = inputs["L0_attn_w_q_down"]
    m["wqd_t"] = np.ascontiguousarray(wqd.reshape(n_dc, 128, n_qc, 128).transpose(2, 1, 0, 3))
    wqun = inputs["L0_attn_w_q_up_nope"]
    m["wqun_t"] = np.ascontiguousarray(wqun.reshape(n_qc, 128, n_heads, 128).transpose(2, 1, 0, 3))
    wqur = inputs["L0_attn_w_q_up_rope"]
    m["wqur_t"] = np.ascontiguousarray(wqur.reshape(n_qc, 128, n_heads, cfg.dr).transpose(2, 1, 0, 3))
    wkv = inputs["L0_attn_w_kv_down"]
    m["wkv_fm"] = np.ascontiguousarray(wkv.reshape(n_dc, 128, n_rc * 128).transpose(1, 0, 2))
    wkr = inputs["L0_attn_w_k_rope"]
    m["wkr_fm"] = np.ascontiguousarray(wkr.reshape(n_dc, 128, cfg.dr).transpose(1, 0, 2))
    wuk = inputs["L0_attn_w_uk"]
    m["wuk_fm"] = np.ascontiguousarray(wuk.reshape(n_heads, 128, n_rc * 128))
    wuv = inputs["L0_attn_w_uv"]
    m["wuv_tt"] = np.ascontiguousarray(wuv.reshape(n_heads, 128, n_rc, 128).transpose(0, 3, 2, 1))
    wout = inputs["L0_attn_w_out"]
    m["wout_t"] = np.ascontiguousarray(wout.reshape(n_heads, 128, n_dc, 128).transpose(2, 1, 0, 3))
    pos = np.concatenate(posl)
    c_, s_ = yarn_tables(cfg, pos)
    m["cos_q_h"] = np.ascontiguousarray(c_)
    m["sin_q_h"] = np.ascontiguousarray(s_)
    thresh = cfg.GA + (core % cfg.n_mem) * TT
    pk = np.arange(128)[:, None] + np.arange(cfg.GLEN)[None, :]
    m["G_h"] = np.where(pk <= thresh, 0.0, -30000.0).astype(np.float32)
    return m


def host_unprep(cfg, y_fm):
    return np.ascontiguousarray(y_fm.transpose(0, 3, 2, 1).reshape(cfg.T, cfg.D))


from concourse.bass_utils import run_bass_kernel_spmd


def _host_shared(cfg, inputs):
    n_dc, n_ic, n_qc, n_rc, n_heads = cfg.n_dc, cfg.n_ic, cfg.n_qc, cfg.n_rc, cfg.n_heads
    m = {}
    gate = inputs["L0_ffn_gate_wq"]; up = inputs["L0_ffn_up_wq"]; down = inputs["L0_ffn_down_wq"]
    gt = gate.reshape(n_ic, 128, n_dc, 128)
    ut = up.reshape(n_ic, 128, n_dc, 128)
    guw = np.stack([gt, ut], axis=1)
    m["guw_t"] = np.ascontiguousarray(guw.transpose(0, 4, 1, 3, 2))
    dt_ = down.reshape(cfg.n_hc, 128, n_ic, 128)
    m["down_t"] = np.ascontiguousarray(dt_.transpose(0, 3, 2, 1))
    gs = inputs["L0_ffn_gate_scale"]; us = inputs["L0_ffn_up_scale"]; ds = inputs["L0_ffn_down_scale"]
    m["gs_us_h"] = np.ascontiguousarray((gs * us).reshape(n_ic, 128).T)
    m["ds_half_h"] = np.ascontiguousarray((0.5 * ds).reshape(cfg.n_hc, 128).T)
    m["wqd_t"] = np.ascontiguousarray(inputs["L0_attn_w_q_down"].reshape(n_dc, 128, n_qc, 128).transpose(2, 1, 0, 3))
    m["wqun_t"] = np.ascontiguousarray(inputs["L0_attn_w_q_up_nope"].reshape(n_qc, 128, n_heads, 128).transpose(2, 1, 0, 3))
    m["wqur_t"] = np.ascontiguousarray(inputs["L0_attn_w_q_up_rope"].reshape(n_qc, 128, n_heads, cfg.dr).transpose(2, 1, 0, 3))
    m["wkv_fm"] = np.ascontiguousarray(inputs["L0_attn_w_kv_down"].reshape(n_dc, 128, n_rc * 128).transpose(1, 0, 2))
    m["wkr_fm"] = np.ascontiguousarray(inputs["L0_attn_w_k_rope"].reshape(n_dc, 128, cfg.dr).transpose(1, 0, 2))
    m["wuk_fm"] = np.ascontiguousarray(inputs["L0_attn_w_uk"].reshape(n_heads, 128, n_rc * 128))
    m["wuv_tt"] = np.ascontiguousarray(inputs["L0_attn_w_uv"].reshape(n_heads, 128, n_rc, 128).transpose(0, 3, 2, 1))
    m["wout_t"] = np.ascontiguousarray(inputs["L0_attn_w_out"].reshape(n_heads, 128, n_dc, 128).transpose(2, 1, 0, 3))
    return m


def _host_core(cfg, inputs, core):
    n_dc, TT, NT, T = cfg.n_dc, cfg.TT, cfg.NT, cfg.T
    b = core // cfg.n_mem
    posl = core_positions(cfg, core)
    x = inputs["L0_attn_hidden_states"]
    tok = np.concatenate([np.asarray(x[b, p[0]:p[0] + TT, :]) for p in posl], axis=0)
    m = {}
    m["x_fm"] = np.ascontiguousarray(tok.reshape(NT, TT, n_dc, 128).transpose(0, 3, 2, 1))
    pos = np.concatenate(posl)
    c_, s_ = yarn_tables(cfg, pos)
    m["cos_q_h"] = np.ascontiguousarray(c_)
    m["sin_q_h"] = np.ascontiguousarray(s_)
    thresh = cfg.GA + (core % cfg.n_mem) * TT
    pk = np.arange(128)[:, None] + np.arange(cfg.GLEN)[None, :]
    m["G_h"] = np.where(pk <= thresh, 0.0, -30000.0).astype(np.float32)
    return m


def kernel(**inputs):
    cfg = Cfg()
    nc = build_program(cfg)
    shared = _host_shared(cfg, inputs)
    in_maps = []
    for c in range(8):
        m = dict(shared)
        m.update(_host_core(cfg, inputs, c))
        in_maps.append(m)
    res = run_bass_kernel_spmd(nc, in_maps, core_ids=list(range(8)))
    out = np.empty((2, 4096, 4096), dtype=np.float32)
    for c in range(8):
        y = np.asarray(res.results[c]["y_fm"]).reshape(cfg.NT, 128, cfg.n_dc, cfg.TT)
        b = c // cfg.n_mem
        rows = host_unprep(cfg, y)
        posl = core_positions(cfg, c)
        for ts in range(cfg.NT):
            o = int(posl[ts][0])
            out[b, o:o + cfg.TT, :] = rows[ts * cfg.TT:(ts + 1) * cfg.TT]
    return out
